# version 03
# baseline (speedup 1.0000x reference)
import contextlib
import numpy as np
import ml_dtypes
import concourse.bass as bass
import concourse.mybir as mybir
from concourse.bass_utils import run_bass_kernel_spmd

F32 = mybir.dt.float32
BF16 = mybir.dt.bfloat16
AF = mybir.ActivationFunctionType
ALU = mybir.AluOpType

D = 2048
DC = 16
TC = 2048
B_, T_ = 2, 8192
DI = 4096
DH = 8192
NH, HDIM = 16, 128
HALO = 3
TW = TC + HALO
LNEPS = 1e-5
LOG_RTD = float(np.log(np.sqrt(2048.0)))
INVSQ = 1.0 / float(np.sqrt(128.0))
INVN_H = 1.0 / float(np.sqrt(8192.0))
BIGNEG = -1e30
GROUPS = [[0, 1, 2, 3], [4, 5, 6, 7]]

bf16 = ml_dtypes.bfloat16


def _pm_matrix(n):
    a = np.arange(n)
    m = a[:, None] & a[None, :]
    pc = np.zeros((n, n), dtype=np.int64)
    v = m.copy()
    while v.any():
        pc += v & 1
        v >>= 1
    return np.where(pc % 2 == 0, 1.0, -1.0).astype(np.float32)


def _host_consts():
    c = {}
    c["ident_bf"] = np.eye(128, dtype=bf16)
    c["ident_f"] = np.eye(128, dtype=np.float32)
    c["ones_col_bf"] = np.ones((128, 1), dtype=bf16)
    c["ones_row_f"] = np.ones((1, 128), dtype=np.float32)
    c["h128_bf"] = _pm_matrix(128).astype(bf16)
    c["h64_bf"] = _pm_matrix(64).astype(bf16)
    theta = 1.0 / (10000.0 ** (np.arange(64, dtype=np.float64) / 64))
    fr = np.outer(theta, np.arange(T_, dtype=np.float64))
    c["cos_t"] = np.cos(fr).astype(np.float32)
    c["sin_t"] = np.sin(fr).astype(np.float32)
    jj = np.arange(1024)[None, :]
    pp = np.arange(128)[:, None]
    c["dwide"] = (jj - pp).astype(np.float32)
    mw = np.zeros((4, 128, 1024), dtype=np.float32)
    mn = np.zeros((4, 128, 512), dtype=np.float32)
    for ci in range(4):
        mw[ci] = np.where(jj <= (512 + 128 * ci) + pp, 0.0, BIGNEG)
        mn[ci] = np.where(jj[:, :512] <= (128 * ci) + pp, 0.0, BIGNEG)
    c["mask_w"] = np.ascontiguousarray(mw.transpose(1, 0, 2)).astype(bf16)
    c["mask_n"] = np.ascontiguousarray(mn.transpose(1, 0, 2)).astype(bf16)
    cb = np.zeros((128, 4), dtype=np.float32)
    cb[:, 0] = LNEPS
    cb[:, 1] = LOG_RTD
    c["cbias"] = cb
    return c


_DVALS = (np.arange(-63, 1) * 128).astype(np.int64)
_DMAP = {int(v): i for i, v in enumerate(_DVALS)}


def _per_core_tables(g):
    heads = [4 * g + u for u in range(4)]
    slopes = np.array([2.0 ** (-8.0 * (h + 1) / NH) for h in heads], dtype=np.float64)
    t = {"slope": np.broadcast_to(slopes.astype(np.float32), (128, 4)).copy()}
    db = np.zeros((4, 64), dtype=np.float64)
    for u in range(4):
        db[u] = slopes[u] * _DVALS
    t["dbias"] = np.broadcast_to(db.reshape(1, 256).astype(np.float32), (128, 256)).copy()
    return t


class Stream:

    def __init__(self, mk, name, qname, nslots):
        self.mk = mk
        self.qname = qname
        self.n = nslots
        self.names = [f"{name}{i}" for i in range(nslots)]
        for nm in self.names:
            mk.sem(nm)
        self.i = 0

    def issue(self, fn, gate=None):
        mk = self.mk
        q = getattr(mk.nc, self.qname)
        s = self.i % self.n
        nm = self.names[s]
        if gate is not None:
            for gh, gv in ([gate] if isinstance(gate, tuple) else gate):
                q.wait_ge(gh, gv)
        if mk.cnt[nm] > 0:
            q.wait_ge(mk.sems[nm], mk.cnt[nm])
        ins = fn(q)
        mk.inc(ins, nm, 16)
        self.i += 1
        return (mk.sems[nm], mk.cnt[nm])


class MK:
    def __init__(self, taps=(), detect_races=True):
        self.nc = bass.Bass("TRN2", target_bir_lowering=False,
                            detect_race_conditions=detect_races)
        self.taps = list(taps)
        self.es = contextlib.ExitStack()
        self.sems = {}
        self.cnt = {}
        self.dram = {}
        self.out_names = []
        self.streams = {}
        self.dirty = set()
        self.phase_no = 0
        self.carry_ctx = None

    def sem(self, name):
        h = self.es.enter_context(self.nc.semaphore(name))
        self.sems[name] = h
        self.cnt[name] = 0
        return h

    def inc(self, instr, name, n=1):
        instr.then_inc(self.sems[name], n)
        self.cnt[name] += n
        self.dirty.add(name)
        return self.cnt[name]

    def tok(self, name):
        return (self.sems[name], self.cnt[name])

    def wtok(self, eng, token):
        eng.wait_ge(token[0], token[1])

    def wait(self, eng, name, v=None):
        eng.wait_ge(self.sems[name], self.cnt[name] if v is None else v)

    def stream(self, name, qname, nslots):
        if name not in self.streams:
            self.streams[name] = Stream(self, name, qname, nslots)
        st = self.streams[name]
        assert st.qname == qname
        return st

    def barrier(self):
        nc = self.nc
        for name in sorted(self.dirty):
            if name != "phase":
                nc.sync.wait_ge(self.sems[name], self.cnt[name])
        self.dirty.clear()
        self.phase_no += 1
        self.inc(nc.sync.nop(), "phase", 1)
        for eng in (nc.tensor, nc.vector, nc.scalar, nc.gpsimd):
            eng.wait_ge(self.sems["phase"], self.phase_no)

    def din(self, name, shape, dtype=F32):
        ap = self.nc.dram_tensor(name, list(shape), dtype, kind="ExternalInput").ap()
        self.dram[name] = ap
        return ap

    def dout(self, name, shape, dtype=F32):
        ap = self.nc.dram_tensor(name, list(shape), dtype, kind="ExternalOutput").ap()
        self.dram[name] = ap
        self.out_names.append(name)
        return ap

    def dscratch(self, name, shape, dtype=F32):
        ap = self.nc.dram_tensor(name, list(shape), dtype).ap()
        self.dram[name] = ap
        return ap

    def sb(self, name, shape, dtype=F32, stack=None, side=None):
        return (stack or self.es).enter_context(
            self.nc.sbuf_tensor(name, list(shape), dtype, side=side))

    def ps(self, name, shape, dtype=F32, stack=None):
        return (stack or self.es).enter_context(
            self.nc.psum_tensor(name, list(shape), dtype))

    def carry(self, name, shape, dtype):
        assert self.carry_ctx is None
        self.carry_ctx = self.nc.sbuf_tensor(name, list(shape), dtype, side="right")
        return self.carry_ctx.__enter__()

    def carry_free(self):
        self.carry_ctx.__exit__(None, None, None)
        self.carry_ctx = None

    def bcast(self, psum_ap, row_ap):
        return self.nc.tensor.matmul(psum_ap, self.c_ones_row[:], row_ap,
                                     start=True, stop=True)

    def emit_taps(self):
        nc = self.nc
        for tname in self.taps:
            src = self.dram[tname]
            dst = self.nc.dram_tensor("tap_" + tname, list(src.shape), src.dtype,
                                      kind="ExternalOutput").ap()
            self.out_names.append("tap_" + tname)
            self.inc(nc.sync.dma_start(out=dst, in_=src), "store", 16)

    def build(self, phase_limit=99):
        nc = self.nc
        x_own = self.din("x_own", [TC, D])
        x_halo = self.din("x_halo", [128, D])
        w_inT = self.din("w_inT", [D, 2 * DI])
        conv_w = self.din("conv_w", [128, 32, 4])
        conv_b = self.din("conv_b", [128, 32, 1])
        w_outT = self.din("w_outT", [DI, D])
        mlp1T = self.din("mlp1T", [D, DH])
        mlp2T = self.din("mlp2T", [DH, D])
        w1T = self.din("w1T", [D, DH])
        w3T = self.din("w3T", [D, DH])
        w2T = self.din("w2T", [DH, D])
        wq_own = self.din("wq_own", [D, 512])
        wk_own = self.din("wk_own", [D, HDIM])
        wv_own = self.din("wv_own", [D, HDIM])
        wo_own = self.din("wo_own", [512, D])
        ident_bf = self.din("ident_bf", [128, 128], BF16)
        ident_f = self.din("ident_f", [128, 128], F32)
        ones_col_bf = self.din("ones_col_bf", [128, 1], BF16)
        ones_row_f = self.din("ones_row_f", [1, 128], F32)
        h128_bf = self.din("h128_bf", [128, 128], BF16)
        h64_bf = self.din("h64_bf", [64, 64], BF16)
        cos_t = self.din("cos_t", [64, T_])
        sin_t = self.din("sin_t", [64, T_])
        dwide = self.din("dwide", [128, 1024])
        mask_w = self.din("mask_w", [128, 4, 1024], BF16)
        mask_n = self.din("mask_n", [128, 4, 512], BF16)
        slope = self.din("slope", [128, 4])
        dbias = self.din("dbias", [128, 256])
        cbias = self.din("cbias", [128, 4])

        out = self.dout("out", [TC, D], F32)

        r0_d = self.dscratch("r0", [DC, 128, TC])
        xa0_d = self.dscratch("xa0", [DC, 128, TC])
        y_d = self.dscratch("y", [DI, TC], BF16)
        xa1_d = self.dscratch("xa1", [DC, 128, TC])
        r_d = self.dscratch("rrelu", [DH, TC], BF16)
        x1_d = self.dscratch("x1", [DC, 128, TC])
        hid_d = self.dscratch("hid", [DH, TC], BF16)
        x2_d = self.dscratch("x2", [DC, 128, TC])
        x3_d = self.dscratch("x3", [DC, 128, TC])
        x4_d = self.dscratch("x4", [DC, 128, TC])
        w2y_d = self.dscratch("w2y", [DH, D], BF16)
        w2p_d = self.dscratch("w2p", [DH, D], BF16)
        q_d = self.dscratch("qh", [4, 128, T_], BF16)
        attn_d = self.dscratch("attn", [512, T_], BF16)
        xg_in = self.dscratch("xg_in", [D, TC], BF16)
        xg_out = self.dscratch("xg_out", [4 * D, TC], BF16)
        rs_in = self.dscratch("rs_in", [4 * D, TC])
        rs_out = self.dscratch("rs_out", [D, TC])

        for s in ["mm", "ev", "sc", "gp", "cc", "phase", "store", "ldc"]:
            self.sem(s)

        self.c_ident_bf = self.sb("c_ident_bf", [128, 128], BF16, side="right")
        self.c_ident_f = self.sb("c_ident_f", [128, 128], F32, side="right")
        self.c_ones_bf = self.sb("c_ones_bf", [128, 1], BF16, side="right")
        self.c_ones_row = self.sb("c_ones_row", [1, 128], F32, side="right")
        self.c_bias = self.sb("c_bias", [128, 4], F32, side="right")
        self.c_wtap = self.sb("c_wtap", [128, 32, 4], F32, side="right")
        self.c_wbias = self.sb("c_wbias", [128, 32, 1], F32, side="right")
        self.c_h128 = self.sb("c_h128", [128, 128], BF16, side="right")
        self.c_h64 = self.sb("c_h64", [64, 64], BF16, side="right")
        self.c_dwide = self.sb("c_dwide", [128, 1024], F32, side="right")
        self.c_maskw = self.sb("c_maskw", [128, 4, 1024], BF16, side="right")
        self.c_maskn = self.sb("c_maskn", [128, 4, 512], BF16, side="right")
        self.c_slope = self.sb("c_slope", [128, 4], F32, side="right")
        self.c_dbias = self.sb("c_dbias", [128, 256], F32, side="right")
        self.lnrows_d = self.dscratch("lnrows", [6, TC])

        for dst, src in [(self.c_ident_bf, ident_bf), (self.c_ident_f, ident_f),
                         (self.c_ones_bf, ones_col_bf), (self.c_ones_row, ones_row_f),
                         (self.c_bias, cbias),
                         (self.c_wtap, conv_w), (self.c_wbias, conv_b),
                         (self.c_h128, h128_bf), (self.c_h64, h64_bf),
                         (self.c_dwide, dwide), (self.c_maskw, mask_w),
                         (self.c_maskn, mask_n), (self.c_slope, slope),
                         (self.c_dbias, dbias)]:
            self.inc(nc.sync.dma_start(out=dst[:], in_=src), "ldc", 16)
        for eng in (nc.tensor, nc.vector, nc.scalar, nc.gpsimd):
            self.wait(eng, "ldc")

        self.p1a(x_own, x_halo, r0_d, xa0_d)
        self.barrier()
        if phase_limit >= 2:
            self.p1b(w_inT, y_d)
            self.barrier()
        if phase_limit >= 3:
            h2_d = self.dscratch("h2", [DC, 128, TC], BF16)
            self.p1c_a(w_outT, y_d, xa0_d, xa1_d, h2_d)
            self.p1c_b(mlp1T, h2_d, r_d)
            self.barrier()
            self.contract8k("mlp2", mlp2T, False, r_d, xa1_d, r0_d, x1_d,
                            stats_row=0)
            self.barrier()
        if phase_limit >= 4:
            self.fold_w2(w2T, w2y_d, w2p_d)
            self.barrier()
            self.ffn("f1", x1_d, w1T, w3T, w2p_d, hid_d, x2_d,
                     ln_row_in=0, stats_row=2)
            self.barrier()
        if phase_limit >= 5:
            self.p3_gather(x2_d, xg_in, xg_out, ln_row=2)
        if phase_limit >= 6:
            self.p4_qkv(xg_out, wq_own, wk_own, wv_own, cos_t, sin_t, q_d)
            self.barrier()
            self.p5_attn(q_d, attn_d)
            self.barrier()
            self.carry_free()
        if phase_limit >= 7:
            self.p6_wo(wo_own, attn_d, rs_in, rs_out)
            self.p6b_x3(x2_d, rs_out, x3_d, stats_row=4)
            self.barrier()
            self.ffn("f2", x3_d, w1T, w3T, w2p_d, hid_d, x4_d,
                     ln_row_in=4, stats_row=None)
            self.barrier()
        if phase_limit >= 8:
            self.p8_out(x4_d, out)
            self.barrier()

        self.emit_taps()
        for name in sorted(set(list(self.dirty) + ["store"])):
            if name != "phase":
                self.wait(nc.gpsimd, name)
        if self.carry_ctx is not None:
            self.carry_free()
        return self

    def ln_rows(self, kind, psum_sum, psum_sq, m_ap, r_ap, s1, s2):
        nc = self.nc
        if kind == "ln":
            nc.vector.tensor_scalar(m_ap, psum_sum, 1.0 / D, None, ALU.mult)
            nc.vector.tensor_scalar(s1, psum_sq, 1.0 / D, None, ALU.mult)
            nc.vector.tensor_tensor(s2, m_ap, m_ap, ALU.mult)
            ins = nc.vector.tensor_tensor(s1, s1, s2, ALU.subtract)
            vv = self.inc(ins, "ev", 1)
            self.wait(nc.scalar, "ev", vv)
            nc.scalar.activation(out=s2, in_=s1, func=AF.Ln,
                                 bias=self.c_bias[0:1, 0:1])
            ins = nc.scalar.activation(out=r_ap, in_=s2, func=AF.Exp, scale=-0.5)
            return self.inc(ins, "sc", 1)
        else:
            ins = nc.vector.tensor_scalar(s1, psum_sq, 1e-24, None, ALU.max)
            vv = self.inc(ins, "ev", 1)
            self.wait(nc.scalar, "ev", vv)
            nc.scalar.activation(out=s2, in_=s1, func=AF.Ln)
            ins = nc.scalar.activation(out=r_ap, in_=s2, func=AF.Exp,
                                       scale=-0.5, bias=self.c_bias[0:1, 1:2])
            return self.inc(ins, "sc", 1)

    def p1a(self, x_own, x_halo, r0_d, xa0_d):
        nc = self.nc
        st = contextlib.ExitStack()
        self.h_sb = self.carry("h", [128, DC, TW], BF16)
        with st:
            xt = [self.sb(f"p1a_xt{i}", [128, D], F32, st) for i in range(2)]
            xs = self.sb("p1a_xs", [128, DC, 128], F32, st)
            xs_bf = self.sb("p1a_xsbf", [128, DC, 128], BF16, st)
            sq_bf = self.sb("p1a_sqbf", [128, DC, 128], BF16, st)
            tmp = self.sb("p1a_tmp", [128, DC, 128], F32, st)
            xa0s = self.sb("p1a_xa0", [128, DC, 128], F32, st)
            ptr = [self.ps(f"p1a_ptr{i}", [128, 512], F32, st) for i in range(2)]
            pst = [self.ps(f"p1a_pst{i}", [1, 128], F32, st) for i in range(2)]
            pbc = self.ps("p1a_pbc", [128, 256], F32, st)
            lrow = self.sb("p1a_rows", [1, 5, 128], F32, st)
            ld = self.stream("Lxt", "sync", 2)
            sp = self.stream("Sxa", "gpsimd", 4)

            mm_after = {}
            ev_after = {}
            sp_r0 = {}
            sp_xa = {}
            for gi in range(17):
                buf = xt[gi % 2]
                gate = (self.sems["mm"], mm_after[gi - 2]) if gi >= 2 else None
                if gi == 0:
                    t = ld.issue(lambda q: q.dma_start(out=buf[:], in_=x_halo),
                                 gate=gate)
                else:
                    t = ld.issue(lambda q, g=gi: q.dma_start(
                        out=buf[:], in_=x_own[(g - 1) * 128: g * 128, :]), gate=gate)

                self.wtok(nc.tensor, t)
                if gi >= 1:
                    self.wait(nc.tensor, "sc")
                for half in range(4):
                    pb = ptr[half % 2]
                    if gi * 4 + half >= 2:
                        self.wait(nc.tensor, "ev", ev_after[gi * 4 + half - 2])
                    for k in range(4):
                        dc = half * 4 + k
                        _ti = nc.tensor.transpose(pb[:, k * 128:(k + 1) * 128],
                                            buf[:, dc * 128:(dc + 1) * 128],
                                            self.c_ident_f[:])
                    mmv = self.inc(_ti, "mm", 1)
                    self.wait(nc.vector, "mm", mmv)
                    if gi >= 2 and half == 0:
                        self.wtok(nc.vector, sp_r0[gi - 1])
                    ins = nc.vector.tensor_copy(xs[:, half * 4:(half + 1) * 4, :], pb[:])
                    ev_after[gi * 4 + half] = self.inc(ins, "ev", 1)
                mm_after[gi] = self.cnt["mm"]

                self.wait(nc.scalar, "ev")
                ins = nc.scalar.activation(out=sq_bf[:], in_=xs[:], func=AF.Square)
                sqv = self.inc(ins, "sc", 1)
                ins = nc.vector.tensor_copy(xs_bf[:], xs[:])
                xbv = self.inc(ins, "ev", 1)

                nc.tensor.wait_ge(self.sems["ev"], xbv)
                nc.tensor.wait_ge(self.sems["sc"], sqv)
                for dc in range(DC):
                    _ti = nc.tensor.matmul(pst[0][:], self.c_ones_bf[:], xs_bf[:, dc, :],
                                     start=(dc == 0), stop=(dc == DC - 1))
                for dc in range(DC):
                    _ti = nc.tensor.matmul(pst[1][:], self.c_ones_bf[:], sq_bf[:, dc, :],
                                     start=(dc == 0), stop=(dc == DC - 1))
                mmv = self.inc(_ti, "mm", 1)
                self.wait(nc.vector, "mm", mmv)
                scv = self.ln_rows("ln", pst[0][:], pst[1][:],
                                   lrow[:, 0, :], lrow[:, 1, :],
                                   lrow[:, 3, :], lrow[:, 4, :])

                nc.tensor.wait_ge(self.sems["sc"], scv)
                _ti = self.bcast(pbc[:, 0:128], lrow[:, 0, :])
                _ti = self.bcast(pbc[:, 128:256], lrow[:, 1, :])
                mmv = self.inc(_ti, "mm", 1)
                self.wait(nc.vector, "mm", mmv)
                if gi >= 2:
                    self.wtok(nc.vector, sp_xa[gi - 1])
                mb = pbc[:, 0:128].unsqueeze(1).broadcast_to([128, DC, 128])
                rb = pbc[:, 128:256].unsqueeze(1).broadcast_to([128, DC, 128])
                nc.vector.tensor_tensor(tmp[:], xs[:], mb, ALU.subtract)
                ins = nc.vector.tensor_tensor(xa0s[:], tmp[:], rb, ALU.mult)
                xav = self.inc(ins, "ev", 1)

                self.wait(nc.scalar, "ev", xav)
                ins = nc.scalar.activation(out=sq_bf[:], in_=xa0s[:], func=AF.Square)
                sqv = self.inc(ins, "sc", 1)
                nc.tensor.wait_ge(self.sems["sc"], sqv)
                for dc in range(DC):
                    _ti = nc.tensor.matmul(pst[0][:], self.c_ones_bf[:], sq_bf[:, dc, :],
                                     start=(dc == 0), stop=(dc == DC - 1))
                mmv = self.inc(_ti, "mm", 1)
                self.wait(nc.vector, "mm", mmv)
                scv = self.ln_rows("rms", None, pst[0][:],
                                   None, lrow[:, 2, :],
                                   lrow[:, 3, :], lrow[:, 4, :])
                nc.tensor.wait_ge(self.sems["sc"], scv)
                _ti = self.bcast(pbc[:, 0:128], lrow[:, 2, :])
                mmv = self.inc(_ti, "mm", 1)
                self.wait(nc.vector, "mm", mmv)
                sb_ = pbc[:, 0:128].unsqueeze(1).broadcast_to([128, DC, 128])
                if gi == 0:
                    ins = nc.vector.tensor_tensor(
                        self.h_sb[:, :, 0:HALO], xa0s[:, :, 0:HALO],
                        pbc[:, 0:HALO].unsqueeze(1).broadcast_to([128, DC, HALO]),
                        ALU.mult)
                else:
                    ins = nc.vector.tensor_tensor(
                        self.h_sb[:, :, HALO + (gi - 1) * 128: HALO + gi * 128],
                        xa0s[:], sb_, ALU.mult)
                hv = self.inc(ins, "ev", 1)

                if gi >= 1:
                    tsl = slice((gi - 1) * 128, gi * 128)
                    sp_r0[gi] = sp.issue(
                        lambda q, s=tsl: q.dma_start(
                            out=r0_d.rearrange("dc p t -> p dc t")[:, :, s], in_=xs[:]),
                        gate=(self.sems["ev"], ev_after[gi * 4 + 3]))
                    sp_xa[gi] = sp.issue(
                        lambda q, s=tsl: q.dma_start(
                            out=xa0_d.rearrange("dc p t -> p dc t")[:, :, s], in_=xa0s[:]),
                        gate=(self.sems["ev"], xav))
                nc.tensor.wait_ge(self.sems["ev"], hv)

            if "h_dump" in self.taps:
                h_d = self.dscratch("h_dump", [128, DC, TW], BF16)
                sp.issue(lambda q: q.dma_start(out=h_d, in_=self.h_sb[:]),
                         gate=self.tok("ev"))

    def p1b(self, w_inT, y_d):
        nc = self.nc
        st = contextlib.ExitStack()
        tcs = [(0, 512), (512, 512), (1024, 512), (1536, 512), (2048, 3)]
        PW = 256
        with st:
            wps = [self.sb(f"p1b_w{i}", [128, DC, PW], BF16, st) for i in range(2)]
            xp = self.sb("p1b_xp", [128, 2, TW], F32, st)
            sz = self.sb("p1b_sz", [128, 2, TW], F32, st)
            xc = self.sb("p1b_xc", [128, TC], F32, st)
            ctmp = self.sb("p1b_ctmp", [128, TC], F32, st)
            sxc = self.sb("p1b_sxc", [128, TC], F32, st)
            yb = [self.sb(f"p1b_yb{i}", [128, TC], BF16, st) for i in range(2)]
            pm = [self.ps(f"p1b_pm{i}", [128, 512], F32, st) for i in range(2)]
            wst = self.stream("Lw", "gpsimd", 3)
            yst = self.stream("Sy", "sync", 4)

            wtoks = {}
            mm_idx = {}
            ytoks = {}
            scv_of = {}
            w_r = w_inT.rearrange("(dc p) f -> p dc f", p=128)
            y_w = y_d.rearrange("(cc p) t -> p cc t", p=128)
            NPE = DI // PW

            def wload(idx):
                pe, part = divmod(idx, 2)
                col0 = part * DI + pe * PW
                gate = (self.sems["mm"], mm_idx[idx - 2]) if idx >= 2 else None
                wtoks[idx] = wst.issue(
                    lambda q, c0=col0, b=wps[idx % 2]: q.dma_start(
                        out=b[:], in_=w_r[:, :, c0:c0 + PW]), gate=gate)

            wload(0)
            wload(1)
            gidx = 0
            convv = {}
            for pe in range(NPE):
                for part in range(2):
                    idx = pe * 2 + part
                    wbuf = wps[idx % 2]
                    dst = xp if part == 0 else sz
                    self.wtok(nc.tensor, wtoks[idx])
                    if pe >= 1 and part == 0:
                        self.wait(nc.tensor, "ev", convv[pe - 1])
                    for m in range(2):
                        for (toff, twid) in tcs:
                            pb = pm[gidx % 2]
                            if gidx >= 2:
                                self.wait(nc.tensor, "sc", scv_of[gidx - 2])
                            for dc in range(DC):
                                _ti = nc.tensor.matmul(
                                    pb[:, 0:twid],
                                    wbuf[:, dc, m * 128:(m + 1) * 128],
                                    self.h_sb[:, dc, toff:toff + twid],
                                    start=(dc == 0), stop=(dc == DC - 1))
                            mmv = self.inc(_ti, "mm", 1)
                            self.wait(nc.scalar, "mm", mmv)
                            fn = AF.Identity if part == 0 else AF.Silu
                            ins = nc.scalar.activation(
                                out=dst[:, m, toff:toff + twid], in_=pb[:, 0:twid],
                                func=fn)
                            scv_of[gidx] = self.inc(ins, "sc", 1)
                            gidx += 1
                    mm_idx[idx] = self.cnt["mm"]
                    if idx + 2 < 2 * NPE:
                        wload(idx + 2)

                for m in range(2):
                    cc = pe * 2 + m
                    self.wait(nc.vector, "sc")
                    nc.vector.tensor_scalar(
                        xc[:], xp[:, m, 3:3 + TC], self.c_wtap[:, cc, 3:4],
                        self.c_wbias[:, cc, 0:1], ALU.mult, ALU.add)
                    for j in (2, 1, 0):
                        nc.vector.tensor_scalar(
                            ctmp[:], xp[:, m, j:j + TC], self.c_wtap[:, cc, j:j + 1],
                            None, ALU.mult)
                        ins = nc.vector.tensor_tensor(xc[:], xc[:], ctmp[:], ALU.add)
                    cv = self.inc(ins, "ev", 1)
                    self.wait(nc.scalar, "ev", cv)
                    ins = nc.scalar.activation(out=sxc[:], in_=xc[:], func=AF.Silu)
                    sv = self.inc(ins, "sc", 1)
                    self.wait(nc.vector, "sc", sv)
                    ybuf = yb[cc % 2]
                    if cc >= 2:
                        self.wtok(nc.vector, ytoks[cc - 2])
                    ins = nc.vector.tensor_tensor(ybuf[:], sxc[:], sz[:, m, 3:3 + TC],
                                                  ALU.mult)
                    yv = self.inc(ins, "ev", 1)
                    if m == 1:
                        convv[pe] = yv
                    ytoks[cc] = yst.issue(
                        lambda q, c=cc, b=ybuf: q.dma_start(
                            out=y_w[:, c, :], in_=b[:]),
                        gate=(self.sems["ev"], yv))

    def p1c_a(self, w_outT, y_d, xa0_d, xa1_d, h2_d):
        nc = self.nc
        self.carry_free()
        st = contextlib.ExitStack()
        with st:
            yh = self.sb("p1cA_y", [128, 32, 1024], BF16, st)
            wks = [self.sb(f"p1cA_w{i}", [128, 32, 128], BF16, st) for i in range(2)]
            xa0t = [self.sb(f"p1cA_x{i}", [128, 1024], F32, st) for i in range(3)]
            xa1t = [self.sb(f"p1cA_o{i}", [128, 1024], F32, st) for i in range(2)]
            sqb = self.sb("p1cA_sq", [128, 1024], BF16, st)
            rld = [self.sb(f"p1cA_r{i}", [128, 1024], F32, st) for i in range(3)]
            h2t = [self.sb(f"p1cA_h2{i}", [128, 1024], BF16, st) for i in range(2)]
            lrow = self.sb("p1cA_rows", [1, 3, 1024], F32, st)
            pm = [self.ps(f"p1cA_pm{i}", [128, 512], F32, st) for i in range(2)]
            pstq = [self.ps(f"p1cA_ps{i}", [1, 512], F32, st) for i in range(2)]
            pbc = self.ps("p1cA_pbc", [128, 1024], F32, st)
            yld = self.stream("Lyh", "sync", 2)
            wst = self.stream("Lw", "gpsimd", 3)
            xst = self.stream("Lx", "sync", 3)
            sp = self.stream("Sxa", "gpsimd", 4)
            rst = self.stream("Lr", "sync", 3)
            wout_r = w_outT.rearrange("(cc p) d -> p cc d", p=128)
            y_r = y_d.rearrange("(cc p) t -> p cc t", p=128)

            for hf in range(2):
                hsl = slice(hf * 1024, (hf + 1) * 1024)
                ytok = yld.issue(
                    lambda q: q.dma_start(out=yh[:], in_=y_r[:, :, hsl]),
                    gate=(self.sems["mm"], self.cnt["mm"]))
                wtoks = {}
                xtoks = {}
                wmm = {}
                xa1v = {}
                sptoks = {}

                def wload(dc, hf=hf):
                    gate = (self.sems["mm"],
                            wmm[dc - 2] if dc >= 2 else self.cnt["mm"])
                    wtoks[dc] = wst.issue(
                        lambda q, d=dc, b=wks[dc % 2]: q.dma_start(
                            out=b[:], in_=wout_r[:, :, d * 128:(d + 1) * 128]),
                        gate=gate)

                def xload(dc, hf=hf, hsl=hsl):
                    gate = (self.sems["ev"],
                            xa1v[dc - 3] if dc >= 3 else self.cnt["ev"])
                    xtoks[dc] = xst.issue(
                        lambda q, d=dc, b=xa0t[dc % 3]: q.dma_start(
                            out=b[:], in_=xa0_d[d, :, hsl]), gate=gate)

                for i in range(2):
                    wload(i)
                for i in range(3):
                    xload(i)
                for dc in range(DC):
                    self.wtok(nc.tensor, wtoks[dc])
                    self.wtok(nc.tensor, ytok)
                    for tq in range(2):
                        pb = pm[tq]
                        if dc >= 1:
                            self.wait(nc.tensor, "ev", xa1v[dc - 1])
                        for cc in range(32):
                            _ti = nc.tensor.matmul(pb[:], wks[dc % 2][:, cc, :],
                                             yh[:, cc, tq * 512:(tq + 1) * 512],
                                             start=(cc == 0), stop=(cc == 31))
                        mmv = self.inc(_ti, "mm", 1)
                        self.wait(nc.vector, "mm", mmv)
                        self.wtok(nc.vector, xtoks[dc])
                        if dc >= 2:
                            self.wtok(nc.vector, sptoks[dc - 2])
                        ins = nc.vector.tensor_tensor(
                            xa1t[dc % 2][:, tq * 512:(tq + 1) * 512], pb[:],
                            xa0t[dc % 3][:, tq * 512:(tq + 1) * 512], ALU.add)
                        xa1v[dc] = self.inc(ins, "ev", 1)
                    sptoks[dc] = sp.issue(
                        lambda q, d=dc, b=xa1t[dc % 2], s=hsl: q.dma_start(
                            out=xa1_d[d, :, s], in_=b[:]),
                        gate=(self.sems["ev"], xa1v[dc]))
                    self.wait(nc.scalar, "ev", xa1v[dc])
                    if dc >= 1:
                        self.wait(nc.scalar, "mm")
                    ins = nc.scalar.activation(out=sqb[:], in_=xa1t[dc % 2][:],
                                               func=AF.Square)
                    sqv = self.inc(ins, "sc", 1)
                    nc.tensor.wait_ge(self.sems["sc"], sqv)
                    for tq in range(2):
                        _ti = nc.tensor.matmul(pstq[tq][:], self.c_ones_bf[:],
                                         sqb[:, tq * 512:(tq + 1) * 512],
                                         start=(dc == 0), stop=(dc == DC - 1))
                    wmm[dc] = self.inc(_ti, "mm", 1)
                    if dc + 2 < DC:
                        wload(dc + 2)
                    if dc + 3 < DC:
                        xload(dc + 3)

                self.wait(nc.vector, "mm")
                self.ln_rows("rms", None, pstq[0][:], None, lrow[:, 0, 0:512],
                             lrow[:, 1, 0:512], lrow[:, 2, 0:512])
                scv = self.ln_rows("rms", None, pstq[1][:], None, lrow[:, 0, 512:1024],
                                   lrow[:, 1, 512:1024], lrow[:, 2, 512:1024])
                self.wait(nc.tensor, "ev")
                nc.tensor.wait_ge(self.sems["sc"], scv)
                _ti = self.bcast(pbc[:, 0:512], lrow[:, 0, 0:512])
                _ti = self.bcast(pbc[:, 512:1024], lrow[:, 0, 512:1024])
                mmv = self.inc(_ti, "mm", 1)
                rtoks = {}
                h2v = {}

                def rload(dc, hsl=hsl):
                    gate = (self.sems["ev"],
                            h2v[dc - 3] if dc >= 3 else self.cnt["ev"])
                    g2 = sptoks[dc]
                    gate = [gate, g2]
                    rtoks[dc] = rst.issue(
                        lambda q, d=dc, b=rld[dc % 3]: q.dma_start(
                            out=b[:], in_=xa1_d[d, :, hsl]), gate=gate)

                for i in range(3):
                    rload(i)
                h2toks = {}
                for dc in range(DC):
                    self.wait(nc.vector, "mm", mmv)
                    self.wtok(nc.vector, rtoks[dc])
                    if dc >= 2:
                        self.wtok(nc.vector, h2toks[dc - 2])
                    ins = nc.vector.tensor_tensor(
                        h2t[dc % 2][:], rld[dc % 3][:], pbc[:], ALU.mult)
                    h2v[dc] = self.inc(ins, "ev", 1)
                    h2toks[dc] = sp.issue(
                        lambda q, d=dc, b=h2t[dc % 2], s=hsl: q.dma_start(
                            out=h2_d[d, :, s], in_=b[:]),
                        gate=(self.sems["ev"], h2v[dc]))
                    if dc + 3 < DC:
                        rload(dc + 3)
                self.barrier()

    def p1c_b(self, mlp1T, h2_d, r_d):
        nc = self.nc
        st = contextlib.ExitStack()
        with st:
            h2l = self.sb("p1cB_h2", [128, DC, TC], BF16, st)
            wks = [self.sb(f"p1cB_w{i}", [128, DC, 128], BF16, st) for i in range(3)]
            rb = [self.sb(f"p1cB_r{i}", [128, TC], BF16, st) for i in range(2)]
            pm = [[self.ps(f"p1cB_pm{s}_{i}", [128, 512], F32, st) for i in range(4)]
                  for s in range(2)]
            wst = self.stream("Lw", "gpsimd", 3)
            rstr = self.stream("Sy", "sync", 4)
            h2ld = self.stream("Lyh", "sync", 2)
            w_r = mlp1T.rearrange("(dc p) f -> p dc f", p=128)
            r_w = r_d.rearrange("(m p) t -> p m t", p=128)
            h2tok = h2ld.issue(lambda q: q.dma_start(
                out=h2l[:], in_=h2_d.rearrange("dc p t -> p dc t")))
            self.wtok(nc.tensor, h2tok)

            wtoks = {}
            wmm = {}
            rtoks = {}
            scv_m = {}

            def wload(m):
                gate = (self.sems["mm"],
                        wmm[m - 3] if m >= 3 else self.cnt["mm"])
                wtoks[m] = wst.issue(
                    lambda q, mm_=m, b=wks[m % 3]: q.dma_start(
                        out=b[:], in_=w_r[:, :, mm_ * 128:(mm_ + 1) * 128]), gate=gate)

            for m in range(3):
                wload(m)
            for m in range(64):
                self.wtok(nc.tensor, wtoks[m])
                ps4 = pm[m % 2]
                if m >= 2:
                    self.wait(nc.tensor, "sc", scv_m[m - 2])
                for dc in range(DC):
                    for ci in range(4):
                        _ti = nc.tensor.matmul(ps4[ci][:], wks[m % 3][:, dc, :],
                                               h2l[:, dc, ci * 512:(ci + 1) * 512],
                                               start=(dc == 0), stop=(dc == DC - 1))
                mmv = self.inc(_ti, "mm", 1)
                self.wait(nc.scalar, "mm", mmv)
                if m >= 2:
                    self.wtok(nc.scalar, rtoks[m - 2])
                for ci in range(4):
                    ins = nc.scalar.activation(
                        out=rb[m % 2][:, ci * 512:(ci + 1) * 512], in_=ps4[ci][:],
                        func=AF.Relu)
                scv_m[m] = self.inc(ins, "sc", 1)
                wmm[m] = self.cnt["mm"]
                if m + 3 < 64:
                    wload(m + 3)
                rtoks[m] = rstr.issue(
                    lambda q, mm_=m, b=rb[m % 2]: q.dma_start(
                        out=r_w[:, mm_, :], in_=b[:]),
                    gate=(self.sems["sc"], scv_m[m]))

    def contract8k(self, tag, w_dram, w_is_bf16, act_d, add1_d, add2_d, out_d,
                   stats_row=None):
        nc = self.nc
        st = contextlib.ExitStack()
        mstep = 256 if w_is_bf16 else 128
        nmo = D // mstep
        KC = DH // 128
        with st:
            ah = self.sb(f"c8{tag}_a", [128, KC, 512], BF16, st)
            wks = [self.sb(f"c8{tag}_w{i}", [128, KC, mstep], BF16, st)
                   for i in range(2)]
            a1t = [self.sb(f"c8{tag}_a1{i}", [128, 512], F32, st) for i in range(3)]
            a2t = ([self.sb(f"c8{tag}_a2{i}", [128, 512], F32, st) for i in range(3)]
                   if add2_d is not None else None)
            ot = [self.sb(f"c8{tag}_o{i}", [128, 512], F32, st) for i in range(2)]
            sqb = self.sb(f"c8{tag}_sq", [128, 512], BF16, st)
            xbf = self.sb(f"c8{tag}_xb", [128, 512], BF16, st)
            lrow = self.sb(f"c8{tag}_rows", [1, 4, 512], F32, st)
            pm = [self.ps(f"c8{tag}_pm{i}", [128, 512], F32, st) for i in range(2)]
            pstq = [self.ps(f"c8{tag}_ps{i}", [1, 512], F32, st) for i in range(2)]
            ald = self.stream("Lyh", "sync", 2)
            wst = self.stream("Lw", "gpsimd", 3)
            xst = self.stream("Lx", "sync", 3)
            x2st = self.stream("Lr", "sync", 3) if add2_d is not None else None
            sp = self.stream("Sxa", "gpsimd", 4)
            act_r = act_d.rearrange("(cc p) t -> p cc t", p=128)
            w_rr = w_dram.rearrange("(cc p) d -> p cc d", p=128)

            for qt in range(4):
                qsl = slice(qt * 512, (qt + 1) * 512)
                atok = ald.issue(
                    lambda q: q.dma_start(out=ah[:], in_=act_r[:, :, qsl]),
                    gate=(self.sems["mm"], self.cnt["mm"]))
                wtoks = {}
                wmm = {}
                evv = {}
                a1toks = {}
                a2toks = {}
                sptoks = {}
                statv = {}

                def wload(mo):
                    gate = (self.sems["mm"],
                            wmm[mo - 2] if mo >= 2 else self.cnt["mm"])
                    wtoks[mo] = wst.issue(
                        lambda q, m=mo, b=wks[mo % 2]: q.dma_start(
                            out=b[:], in_=w_rr[:, :, m * mstep:(m + 1) * mstep]),
                        gate=gate)

                def aload(dc):
                    gate = (self.sems["ev"],
                            evv[dc - 3] if dc >= 3 else self.cnt["ev"])
                    a1toks[dc] = xst.issue(
                        lambda q, d=dc, b=a1t[dc % 3]: q.dma_start(
                            out=b[:], in_=add1_d[d, :, qsl]), gate=gate)
                    if add2_d is not None:
                        a2toks[dc] = x2st.issue(
                            lambda q, d=dc, b=a2t[dc % 3]: q.dma_start(
                                out=b[:], in_=add2_d[d, :, qsl]), gate=gate)

                wload(0)
                wload(1)
                for i in range(3):
                    aload(i)
                for mo in range(nmo):
                    self.wtok(nc.tensor, wtoks[mo])
                    self.wtok(nc.tensor, atok)
                    for sub in range(mstep // 128):
                        dc = mo * (mstep // 128) + sub
                        pb = pm[dc % 2]
                        if dc >= 2:
                            self.wait(nc.tensor, "ev", evv[dc - 2])
                        for cc in range(KC):
                            _ti = nc.tensor.matmul(
                                pb[:], wks[mo % 2][:, cc, sub * 128:(sub + 1) * 128],
                                ah[:, cc, :], start=(cc == 0), stop=(cc == KC - 1))
                        mmv = self.inc(_ti, "mm", 1)
                        self.wait(nc.vector, "mm", mmv)
                        self.wtok(nc.vector, a1toks[dc])
                        obuf = ot[dc % 2]
                        if dc >= 2:
                            self.wtok(nc.vector, sptoks[dc - 2])
                        if dc >= 1 and stats_row is not None:
                            self.wait(nc.vector, "mm", statv[dc - 1])
                        ins = nc.vector.tensor_tensor(obuf[:], pb[:], a1t[dc % 3][:],
                                                      ALU.add)
                        if add2_d is not None:
                            self.wtok(nc.vector, a2toks[dc])
                            ins = nc.vector.tensor_tensor(obuf[:], obuf[:],
                                                          a2t[dc % 3][:], ALU.add)
                        if stats_row is not None:
                            ins = nc.vector.tensor_copy(xbf[:], obuf[:])
                        evv[dc] = self.inc(ins, "ev", 1)
                        sptoks[dc] = sp.issue(
                            lambda q, d=dc, b=ot[dc % 2]: q.dma_start(
                                out=out_d[d, :, qsl], in_=b[:]),
                            gate=(self.sems["ev"], evv[dc]))
                        if dc + 3 < DC:
                            aload(dc + 3)
                        if stats_row is not None:
                            self.wait(nc.scalar, "ev", evv[dc])
                            if dc >= 1:
                                self.wait(nc.scalar, "mm", statv[dc - 1])
                            ins = nc.scalar.activation(out=sqb[:], in_=obuf[:],
                                                       func=AF.Square)
                            sqv = self.inc(ins, "sc", 1)
                            nc.tensor.wait_ge(self.sems["sc"], sqv)
                            nc.tensor.wait_ge(self.sems["ev"], evv[dc])
                            _ti = nc.tensor.matmul(
                                pstq[0][:], self.c_ones_bf[:], xbf[:],
                                start=(dc == 0), stop=(dc == DC - 1))
                            _ti = nc.tensor.matmul(
                                pstq[1][:], self.c_ones_bf[:], sqb[:],
                                start=(dc == 0), stop=(dc == DC - 1))
                            statv[dc] = self.inc(_ti, "mm", 1)
                    wmm[mo] = self.cnt["mm"]
                    if mo + 2 < nmo:
                        wload(mo + 2)
                if stats_row is not None:
                    self.wait(nc.vector, "mm")
                    if qt >= 1:
                        self.wtok(nc.vector, self._c8_rowtok)
                    scv = self.ln_rows("ln", pstq[0][:], pstq[1][:],
                                       lrow[:, 0, :], lrow[:, 1, :],
                                       lrow[:, 2, :], lrow[:, 3, :])
                    self._c8_rowtok = sp.issue(
                        lambda q: q.dma_start(
                            out=self.lnrows_d[stats_row:stats_row + 2, qsl],
                            in_=lrow[:, 0:2, :]),
                        gate=(self.sems["sc"], scv))

    def fold_w2(self, w2T, w2y_d, w2p_d):
        nc = self.nc
        st = contextlib.ExitStack()
        with st:
            wt = [self.sb(f"fw_w{i}", [128, D], BF16, st) for i in range(2)]
            ev = [self.sb(f"fw_e{i}", [128, D], BF16, st) for i in range(2)]
            st1 = contextlib.ExitStack()
            st1.__enter__()
            pmm = [self.ps(f"fw_p{i}", [128, 1024], F32, st1) for i in range(2)]
            wst = self.stream("Lw", "gpsimd", 3)
            ost = self.stream("Sy", "sync", 4)
            evv = {}
            otoks = {}
            mmc = {}
            for c in range(64):
                gate = (self.sems["mm"], mmc[c - 2]) if c >= 2 else None
                wtok = wst.issue(
                    lambda q, cc=c, b=wt[c % 2]: q.dma_start(
                        out=b[:], in_=w2T[cc * 128:(cc + 1) * 128, :]), gate=gate)
                self.wtok(nc.tensor, wtok)
                for hn in range(2):
                    pb = pmm[hn]
                    if c >= 1:
                        self.wait(nc.tensor, "ev", evv[c - 1])
                    for nn in range(2):
                        _ti = nc.tensor.matmul(
                            pb[:, nn * 512:(nn + 1) * 512], self.c_h128[:],
                            wt[c % 2][:, hn * 1024 + nn * 512: hn * 1024 + (nn + 1) * 512],
                            start=True, stop=True)
                    mmv = self.inc(_ti, "mm", 1)
                    self.wait(nc.vector, "mm", mmv)
                    if c >= 2 and hn == 0:
                        self.wtok(nc.vector, otoks[c - 2])
                    ins = nc.vector.tensor_copy(
                        ev[c % 2][:, hn * 1024:(hn + 1) * 1024], pb[:])
                    evv[c] = self.inc(ins, "ev", 1)
                mmc[c] = self.cnt["mm"]
                otoks[c] = ost.issue(
                    lambda q, cc=c, b=ev[c % 2]: q.dma_start(
                        out=w2y_d[cc * 128:(cc + 1) * 128, :], in_=b[:]),
                    gate=(self.sems["ev"], evv[c]))
            self.barrier()
            st1.__exit__(None, None, None)

            yst = self.stream("Lyh", "sync", 2)
            yts = [self.sb(f"fw_y{i}", [64, D], BF16, st) for i in range(2)]
            ev2 = [self.sb(f"fw_f{i}", [64, D], BF16, st) for i in range(2)]
            p2 = [self.ps(f"fw_q{i}", [64, D], F32, st) for i in range(2)]
            w2y_r = w2y_d.rearrange("(c p) d -> p c d", p=128)
            w2p_r = w2p_d.rearrange("(c p) d -> p c d", p=128)
            scv2 = {}
            otok2 = {}
            mmp = {}
            for p in range(128):
                gate = (self.sems["mm"], mmp[p - 2]) if p >= 2 else None
                ytok = yst.issue(
                    lambda q, pp=p, b=yts[p % 2]: q.dma_start(
                        out=b[:], in_=w2y_r[pp:pp + 1].squeeze(0)), gate=gate)
                self.wtok(nc.tensor, ytok)
                pb = p2[p % 2]
                if p >= 2:
                    self.wait(nc.tensor, "sc", scv2[p - 2])
                for nn in range(4):
                    _ti = nc.tensor.matmul(pb[:, nn * 512:(nn + 1) * 512], self.c_h64[:],
                                     yts[p % 2][:, nn * 512:(nn + 1) * 512],
                                     start=True, stop=True)
                mmv = self.inc(_ti, "mm", 1)
                mmp[p] = mmv
                self.wait(nc.scalar, "mm", mmv)
                if p >= 2:
                    self.wtok(nc.scalar, otok2[p - 2])
                ins = nc.scalar.activation(out=ev2[p % 2][:], in_=pb[:],
                                           func=AF.Identity, scale=INVN_H)
                scv2[p] = self.inc(ins, "sc", 1)
                otok2[p] = ost.issue(
                    lambda q, pp=p, b=ev2[p % 2]: q.dma_start(
                        out=w2p_r[pp:pp + 1].squeeze(0), in_=b[:]),
                    gate=(self.sems["sc"], scv2[p]))

    def ffn(self, tag, xin_d, w1T, w3T, w2p_d, hid_d, xout_d, ln_row_in, stats_row):
        nc = self.nc
        st = contextlib.ExitStack()
        with st:
            xln = self.sb(f"ff{tag}_xln", [128, DC, TC], BF16, st)
            rows = self.sb(f"ff{tag}_rows", [1, 2, TC], F32, st)
            stA = contextlib.ExitStack()
            stA.__enter__()
            xt = [self.sb(f"ff{tag}_x{i}", [128, DC, 512], F32, stA) for i in range(2)]
            pbc = self.ps(f"ff{tag}_bc", [128, 1024], F32, stA)
            xst = self.stream("Lx", "sync", 3)
            rld = self.stream("Lyh", "sync", 2)
            rtok = rld.issue(lambda q: q.dma_start(
                out=rows[:], in_=self.lnrows_d[ln_row_in:ln_row_in + 2, :]))
            self.wtok(nc.tensor, rtok)
            xin_r = xin_d.rearrange("dc p t -> p dc t")
            xtoks = {}
            apv = {}
            for ci in range(2):
                xtoks[ci] = xst.issue(
                    lambda q, c=ci, b=xt[ci % 2]: q.dma_start(
                        out=b[:], in_=xin_r[:, :, c * 512:(c + 1) * 512]))
            for ci in range(4):
                tsl = slice(ci * 512, (ci + 1) * 512)
                if ci >= 1:
                    self.wait(nc.tensor, "ev")
                _ti = self.bcast(pbc[:, 0:512], rows[:, 0, tsl])
                _ti = self.bcast(pbc[:, 512:1024], rows[:, 1, tsl])
                mmv = self.inc(_ti, "mm", 1)
                self.wait(nc.vector, "mm", mmv)
                self.wtok(nc.vector, xtoks[ci])
                mb = pbc[:, 0:512].unsqueeze(1).broadcast_to([128, DC, 512])
                rb = pbc[:, 512:1024].unsqueeze(1).broadcast_to([128, DC, 512])
                nc.vector.tensor_tensor(xt[ci % 2][:], xt[ci % 2][:], mb, ALU.subtract)
                ins = nc.vector.tensor_tensor(xln[:, :, tsl], xt[ci % 2][:], rb,
                                              ALU.mult)
                apv[ci] = self.inc(ins, "ev", 1)
                if ci + 2 < 4:
                    xtoks[ci + 2] = xst.issue(
                        lambda q, c=ci + 2, b=xt[ci % 2]: q.dma_start(
                            out=b[:], in_=xin_r[:, :, c * 512:(c + 1) * 512]),
                        gate=(self.sems["ev"], apv[ci]))
            self.wait(nc.tensor, "ev")
            stA.__exit__(None, None, None)

            w1b = [self.sb(f"ff{tag}_w1{i}", [128, DC, 128], BF16, st) for i in range(2)]
            w3b = [self.sb(f"ff{tag}_w3{i}", [128, DC, 128], BF16, st) for i in range(2)]
            gat = self.sb(f"ff{tag}_g", [128, TC], F32, st)
            hb = [self.sb(f"ff{tag}_h{i}", [128, TC], BF16, st) for i in range(2)]
            pg = [self.ps(f"ff{tag}_pg{i}", [128, 512], F32, st) for i in range(4)]
            pl = [self.ps(f"ff{tag}_pl{i}", [128, 512], F32, st) for i in range(4)]
            wst = self.stream("Lw", "gpsimd", 3)
            w3st = self.stream("Lw3", "gpsimd", 2)
            hst = self.stream("Sy", "sync", 4)
            w1_r = w1T.rearrange("(dc p) f -> p dc f", p=128)
            w3_r = w3T.rearrange("(dc p) f -> p dc f", p=128)
            hid_w = hid_d.rearrange("(m p) t -> p m t", p=128)
            wtoks = {}
            w3toks = {}
            wmm = {}
            htoks = {}
            scv_m = {}
            evv_m = {}

            def wload(m):
                gate = (self.sems["mm"],
                        wmm[m - 2] if m >= 2 else self.cnt["mm"])
                wtoks[m] = wst.issue(
                    lambda q, mm_=m, b=w1b[m % 2]: q.dma_start(
                        out=b[:], in_=w1_r[:, :, mm_ * 128:(mm_ + 1) * 128]), gate=gate)
                w3toks[m] = w3st.issue(
                    lambda q, mm_=m, b=w3b[m % 2]: q.dma_start(
                        out=b[:], in_=w3_r[:, :, mm_ * 128:(mm_ + 1) * 128]), gate=gate)

            wload(0)
            wload(1)
            for m in range(64):
                self.wtok(nc.tensor, wtoks[m])
                self.wtok(nc.tensor, w3toks[m])
                if m >= 1:
                    self.wait(nc.tensor, "sc", scv_m[m - 1])
                for dc in range(DC):
                    for ci in range(4):
                        _ti = nc.tensor.matmul(pg[ci][:], w1b[m % 2][:, dc, :],
                                               xln[:, dc, ci * 512:(ci + 1) * 512],
                                               start=(dc == 0), stop=(dc == DC - 1))
                gmm = self.inc(_ti, "mm", 1)
                if m >= 1:
                    self.wait(nc.tensor, "ev", evv_m[m - 1])
                for dc in range(DC):
                    for ci in range(4):
                        _ti = nc.tensor.matmul(pl[ci][:], w3b[m % 2][:, dc, :],
                                               xln[:, dc, ci * 512:(ci + 1) * 512],
                                               start=(dc == 0), stop=(dc == DC - 1))
                lmm = self.inc(_ti, "mm", 1)
                self.wait(nc.scalar, "mm", gmm)
                for ci in range(4):
                    ins = nc.scalar.activation(out=gat[:, ci * 512:(ci + 1) * 512],
                                               in_=pg[ci][:], func=AF.Silu)
                scv_m[m] = self.inc(ins, "sc", 1)
                self.wait(nc.vector, "mm", lmm)
                self.wait(nc.vector, "sc", scv_m[m])
                if m >= 2:
                    self.wtok(nc.vector, htoks[m - 2])
                for ci in range(4):
                    tsl = slice(ci * 512, (ci + 1) * 512)
                    ins = nc.vector.tensor_tensor(hb[m % 2][:, tsl], gat[:, tsl],
                                                  pl[ci][:], ALU.mult)
                evv_m[m] = self.inc(ins, "ev", 1)
                wmm[m] = self.cnt["mm"]
                if m + 2 < 64:
                    wload(m + 2)
                htoks[m] = hst.issue(
                    lambda q, mm_=m, b=hb[m % 2]: q.dma_start(
                        out=hid_w[:, mm_, :], in_=b[:]),
                    gate=(self.sems["ev"], evv_m[m]))
        self.barrier()
        self.contract8k(f"{tag}c", w2p_d, True, hid_d, xin_d, None, xout_d,
                        stats_row=stats_row)

    def p3_gather(self, x2_d, xg_in, xg_out, ln_row):
        nc = self.nc
        st = contextlib.ExitStack()
        with st:
            xt = [self.sb(f"p3_x{i}", [128, DC, 512], F32, st) for i in range(2)]
            ob = [self.sb(f"p3_o{i}", [128, DC, 512], BF16, st) for i in range(2)]
            rows = self.sb("p3_rows", [1, 2, TC], F32, st)
            pbc = self.ps("p3_bc", [128, 1024], F32, st)
            xst = self.stream("Lx", "sync", 3)
            rld = self.stream("Lyh", "sync", 2)
            sp = self.stream("Sxa", "gpsimd", 4)
            rtok = rld.issue(lambda q: q.dma_start(
                out=rows[:], in_=self.lnrows_d[ln_row:ln_row + 2, :]))
            self.wtok(nc.tensor, rtok)
            x2_r = x2_d.rearrange("dc p t -> p dc t")
            xg_w = xg_in.rearrange("(dc p) t -> p dc t", p=128)
            xtoks = {}
            ov = {}
            otoks = {}
            for ci in range(2):
                xtoks[ci] = xst.issue(
                    lambda q, c=ci, b=xt[ci % 2]: q.dma_start(
                        out=b[:], in_=x2_r[:, :, c * 512:(c + 1) * 512]))
            for ci in range(4):
                tsl = slice(ci * 512, (ci + 1) * 512)
                if ci >= 1:
                    self.wait(nc.tensor, "ev")
                _ti = self.bcast(pbc[:, 0:512], rows[:, 0, tsl])
                _ti = self.bcast(pbc[:, 512:1024], rows[:, 1, tsl])
                mmv = self.inc(_ti, "mm", 1)
                self.wait(nc.vector, "mm", mmv)
                self.wtok(nc.vector, xtoks[ci])
                if ci >= 2:
                    self.wtok(nc.vector, otoks[ci - 2])
                mb = pbc[:, 0:512].unsqueeze(1).broadcast_to([128, DC, 512])
                rb = pbc[:, 512:1024].unsqueeze(1).broadcast_to([128, DC, 512])
                nc.vector.tensor_tensor(xt[ci % 2][:], xt[ci % 2][:], mb, ALU.subtract)
                ins = nc.vector.tensor_tensor(ob[ci % 2][:], xt[ci % 2][:], rb,
                                              ALU.mult)
                ov[ci] = self.inc(ins, "ev", 1)
                otoks[ci] = sp.issue(
                    lambda q, c=ci, b=ob[ci % 2]: q.dma_start(
                        out=xg_w[:, :, c * 512:(c + 1) * 512], in_=b[:]),
                    gate=(self.sems["ev"], ov[ci]))
                if ci + 2 < 4:
                    xtoks[ci + 2] = xst.issue(
                        lambda q, c=ci + 2, b=xt[ci % 2]: q.dma_start(
                            out=b[:], in_=x2_r[:, :, c * 512:(c + 1) * 512]),
                        gate=(self.sems["ev"], ov[ci]))
        self.barrier()
        ins = nc.gpsimd.collective_compute(
            "AllGather", ALU.bypass, replica_groups=GROUPS,
            ins=[xg_in.opt()], outs=[xg_out.opt()])
        self.inc(ins, "cc", 1)
        for eng in (nc.tensor, nc.vector, nc.scalar, nc.sync):
            self.wait(eng, "cc")

    def p4_qkv(self, xg_out, wq_own, wk_own, wv_own, cos_t, sin_t, q_d):
        nc = self.nc
        self.kv_sb = self.carry("kv", [128, 2, T_], BF16)
        st = contextlib.ExitStack()
        with st:
            trig = self.sb("p4_trig", [128, T_], F32, st)
            wq = self.sb("p4_wq", [128, DC, 512], BF16, st)
            wk = self.sb("p4_wk", [128, DC, 128], BF16, st)
            wv = self.sb("p4_wv", [128, DC, 128], BF16, st)
            xg = [self.sb(f"p4_xg{i}", [128, DC, 512], BF16, st) for i in range(2)]
            rt1 = self.sb("p4_r1", [64, 512], F32, st)
            rt2 = self.sb("p4_r2", [64, 512], F32, st)
            qs = [self.sb(f"p4_qs{i}", [128, 512], BF16, st) for i in range(4)]
            vtmp = self.sb("p4_vtmp", [128, 512], BF16, st)
            pm = [self.ps(f"p4_pm{i}", [128, 512], F32, st) for i in range(2)]
            ptr = self.ps("p4_ptr", [128, 512], BF16, st)
            wst = self.stream("Lw", "gpsimd", 3)
            xst = self.stream("Lx", "sync", 3)
            qsp = self.stream("Sxa", "gpsimd", 4)
            cst = self.stream("Lyh", "sync", 2)

            ct = cst.issue(lambda q: q.dma_start(out=trig[0:64, :], in_=cos_t))
            stk = cst.issue(lambda q: q.dma_start(out=trig[64:128, :], in_=sin_t))
            wqt = wst.issue(lambda q: q.dma_start(
                out=wq[:], in_=wq_own.rearrange("(dc p) f -> p dc f", p=128)))
            wkt = wst.issue(lambda q: q.dma_start(
                out=wk[:], in_=wk_own.rearrange("(dc p) f -> p dc f", p=128)))
            wvt = wst.issue(lambda q: q.dma_start(
                out=wv[:], in_=wv_own.rearrange("(dc p) f -> p dc f", p=128)))
            for t in (ct, stk, wqt, wkt, wvt):
                self.wtok(nc.tensor, t)
                self.wtok(nc.vector, t)

            xg_r = xg_out.rearrange("(r dc p) t -> r p dc t", r=4, p=128)
            xtoks = {}
            qtoks = {}
            mmrt = {}
            trv = {}
            for rt in range(2):
                r, tq = divmod(rt, 4)
                xtoks[rt] = xst.issue(
                    lambda q, rr=r, t0=tq, b=xg[rt % 2]: q.dma_start(
                        out=b[:], in_=xg_r[rr, :, :, t0 * 512:(t0 + 1) * 512]))
            for rt in range(16):
                r, tq = divmod(rt, 4)
                gcol = r * 2048 + tq * 512
                csl = slice(gcol, gcol + 512)
                self.wtok(nc.tensor, xtoks[rt])
                xb = xg[rt % 2]
                for tgt in range(6):
                    pb = pm[tgt % 2]
                    if rt * 6 + tgt >= 2:
                        self.wait(nc.tensor, "ev")
                    if tgt < 4:
                        wsrc = lambda dc: wq[:, dc, tgt * 128:(tgt + 1) * 128]
                    elif tgt == 4:
                        wsrc = lambda dc: wk[:, dc, :]
                    else:
                        wsrc = lambda dc: wv[:, dc, :]
                    for dc in range(DC):
                        _ti = nc.tensor.matmul(pb[:], wsrc(dc), xb[:, dc, :],
                                         start=(dc == 0), stop=(dc == DC - 1))
                    mmv = self.inc(_ti, "mm", 1)
                    self.wait(nc.vector, "mm", mmv)
                    if tgt < 5:
                        if tgt < 4:
                            if rt >= 1:
                                self.wtok(nc.vector, qtoks[(rt - 1) * 4 + tgt])
                            dst = qs[tgt][:]
                        else:
                            dst = self.kv_sb[:, 0, csl]
                        nc.vector.tensor_tensor(rt1[:], pb[0:64, :], trig[0:64, csl],
                                                ALU.mult)
                        nc.vector.tensor_tensor(rt2[:], pb[64:128, :], trig[64:128, csl],
                                                ALU.mult)
                        nc.vector.tensor_tensor(dst[0:64, :], rt1[:], rt2[:],
                                                ALU.subtract)
                        nc.vector.tensor_tensor(rt1[:], pb[0:64, :], trig[64:128, csl],
                                                ALU.mult)
                        nc.vector.tensor_tensor(rt2[:], pb[64:128, :], trig[0:64, csl],
                                                ALU.mult)
                        ins = nc.vector.tensor_tensor(dst[64:128, :], rt1[:], rt2[:],
                                                      ALU.add)
                        vv = self.inc(ins, "ev", 1)
                        if tgt < 4:
                            qtoks[rt * 4 + tgt] = qsp.issue(
                                lambda q, u=tgt, c=csl, b=qs[tgt]: q.dma_start(
                                    out=q_d[u, :, c], in_=b[:]),
                                gate=(self.sems["ev"], vv))
                    else:
                        if rt >= 1:
                            self.wait(nc.vector, "mm", trv[rt - 1])
                        ins = nc.vector.tensor_copy(vtmp[:], pb[:])
                        vv = self.inc(ins, "ev", 1)
                        nc.tensor.wait_ge(self.sems["ev"], vv)
                        for k in range(4):
                            _ti = nc.tensor.transpose(ptr[:, k * 128:(k + 1) * 128],
                                                vtmp[:, k * 128:(k + 1) * 128],
                                                self.c_ident_bf[:])
                        trv[rt] = self.inc(_ti, "mm", 1)
                        self.wait(nc.vector, "mm", trv[rt])
                        ins = nc.vector.tensor_copy(self.kv_sb[:, 1, csl], ptr[:])
                        self.inc(ins, "ev", 1)
                mmrt[rt] = self.cnt["mm"]
                if rt + 2 < 16:
                    r2, tq2 = divmod(rt + 2, 4)
                    xtoks[rt + 2] = xst.issue(
                        lambda q, rr=r2, t0=tq2, b=xg[(rt + 2) % 2]: q.dma_start(
                            out=b[:], in_=xg_r[rr, :, :, t0 * 512:(t0 + 1) * 512]),
                        gate=(self.sems["mm"], mmrt[rt]))

    def p5_attn(self, q_d, attn_d):
        nc = self.nc
        if "kv_dump" in self.taps:
            kvd = self.dscratch("kv_dump", [128, 2, T_], BF16)
            sp0 = self.stream("Sxa", "gpsimd", 4)
            sp0.issue(lambda q: q.dma_start(out=kvd, in_=self.kv_sb[:]),
                      gate=self.tok("ev"))
            self.barrier()
        st = contextlib.ExitStack()
        with st:
            qp = [self.sb(f"p5_qp{i}", [128, 512], BF16, st) for i in range(2)]
            srow = self.sb("p5_srow", [128, 2, T_], F32, st)
            P = self.sb("p5_P", [128, 4, T_], BF16, st)
            pt = [self.sb(f"p5_pt{i}", [128, 512], BF16, st) for i in range(2)]
            ab = self.sb("p5_ab", [128, 1024], F32, st)
            rws = self.sb("p5_rws", [128, 4], F32, st)
            ob = [self.sb(f"p5_ob{i}", [128, 512], BF16, st) for i in range(2)]
            ps_s = [self.ps(f"p5_s{i}", [128, 1024], F32, st) for i in range(2)]
            ps_t = [self.ps(f"p5_t{i}", [128, 512], BF16, st) for i in range(2)]
            ps_o = self.ps("p5_o", [128, 512], F32, st)
            qld = self.stream("Lyh", "sync", 2)
            asp = self.stream("Sxa", "gpsimd", 4)

            ins = nc.vector.memset(rws[:], 0.0)
            rwsv = self.inc(ins, "ev", 1)
            self.wait(nc.scalar, "ev", rwsv)

            otoks = {}
            qtoks = {}
            qmm = {}

            def qload(g):
                u_, p_ = divmod(g, 16)
                gate = (self.sems["mm"],
                        qmm[g - 1] if (g - 1) in qmm else self.cnt["mm"])
                qtoks[g] = qld.issue(
                    lambda q, uu=u_, pp=p_, b=qp[g % 2]: q.dma_start(
                        out=b[:], in_=q_d[uu, :, 512 * pp:512 * (pp + 1)]),
                    gate=gate)

            qload(0)
            qload(1)
            pair_i = 0
            gc = 0
            evac_pair = {}
            exp_scv = {}
            prev_pv_mm = 0
            prev_sov = 0
            for u in range(4):
                ins = nc.gpsimd.tensor_scalar(
                    ab[:], self.c_dwide[:], self.c_slope[:, u:u + 1], None, ALU.mult)
                abv = self.inc(ins, "gp", 1)
                for p in range(16):
                    g = u * 16 + p
                    E = 512 * (p + 1)
                    nkc = E // 128
                    self.wtok(nc.tensor, qtoks[g])
                    for c in range(4):
                        q0 = 128 * c
                        sr = srow[:, gc % 2, :]
                        pairs = []
                        k0 = 0
                        while k0 < E:
                            pairs.append((k0, min(1024, E - k0)))
                            k0 += 1024
                        gpv = 0
                        for (pk, (k0, w)) in enumerate(pairs):
                            pb = ps_s[pair_i % 2]
                            if pair_i >= 2:
                                self.wait(nc.tensor, "ev", evac_pair[pair_i - 2])
                            for kk in range(w // 512):
                                _ti = nc.tensor.matmul(
                                    pb[:, kk * 512:(kk + 1) * 512],
                                    qp[g % 2][:, q0:q0 + 128],
                                    self.kv_sb[:, 0, k0 + kk * 512:k0 + (kk + 1) * 512],
                                    start=True, stop=True)
                            mmv = self.inc(_ti, "mm", 1)
                            self.wait(nc.vector, "mm", mmv)
                            if pk == 0 and gc >= 2:
                                self.wait(nc.vector, "sc", exp_scv[gc - 2])
                            dcol = _DMAP[k0 - (512 * p + 128 * c)] + u * 64
                            ins = nc.vector.tensor_scalar(
                                sr[:, k0:k0 + w], pb[:, 0:w], INVSQ,
                                self.c_dbias[:, dcol:dcol + 1], ALU.mult, ALU.add)
                            evac_pair[pair_i] = self.inc(ins, "ev", 1)
                            self.wait(nc.gpsimd, "ev", evac_pair[pair_i])
                            ins = nc.gpsimd.tensor_tensor(
                                sr[:, k0:k0 + w], sr[:, k0:k0 + w], ab[:, 0:w],
                                ALU.add)
                            if pk == len(pairs) - 1:
                                msk = (self.c_maskw[:, c, :] if w == 1024
                                       else self.c_maskn[:, c, :])
                                ins = nc.gpsimd.tensor_tensor(
                                    sr[:, k0:k0 + w], sr[:, k0:k0 + w], msk, ALU.add)
                            gpv = self.inc(ins, "gp", 1)
                            pair_i += 1
                        self.wait(nc.vector, "gp", gpv)
                        ins = nc.vector.tensor_reduce(
                            rws[:, 0:1], sr[:, 0:E], axis=mybir.AxisListType.X,
                            op=ALU.max, negate=True)
                        nmv = self.inc(ins, "ev", 1)
                        self.wait(nc.scalar, "ev", nmv)
                        self.wait(nc.scalar, "mm", prev_pv_mm)
                        nc.scalar.activation(
                            out=P[:, c, 0:E], in_=sr[:, 0:E], func=AF.Exp,
                            bias=rws[:, 0:1], accum_out=rws[:, 1:2])
                        nc.scalar.activation(out=rws[:, 2:3], in_=rws[:, 1:2],
                                             func=AF.Ln)
                        nc.scalar.activation(out=rws[:, 3:4], in_=rws[:, 2:3],
                                             func=AF.Exp, scale=-1.0)
                        ins = nc.scalar.activation(
                            out=P[:, c, 0:E], in_=P[:, c, 0:E], func=AF.Identity,
                            scale=rws[:, 3:4])
                        exp_scv[gc] = self.inc(ins, "sc", 1)
                        gc += 1

                    qmm[g] = self.cnt["mm"]
                    if g + 2 < 64:
                        qload(g + 2)
                    nc.tensor.wait_ge(self.sems["sc"], exp_scv[gc - 1])
                    ptv = {}
                    for kc in range(nkc + 1):
                        if kc < nkc:
                            pbt = ps_t[kc % 2]
                            if kc >= 2:
                                self.wait(nc.tensor, "ev", ptv[kc - 2])
                            for c4 in range(4):
                                _ti = nc.tensor.transpose(
                                    pbt[:, c4 * 128:(c4 + 1) * 128],
                                    P[:, c4, kc * 128:(kc + 1) * 128],
                                    self.c_ident_bf[:])
                            mmv = self.inc(_ti, "mm", 1)
                            self.wait(nc.vector, "mm", mmv)
                            ins = nc.vector.tensor_copy(pt[kc % 2][:], pbt[:])
                            ptv[kc] = self.inc(ins, "ev", 1)
                        if kc >= 1:
                            k1 = kc - 1
                            nc.tensor.wait_ge(self.sems["ev"], ptv[k1])
                            if k1 == 0:
                                nc.tensor.wait_ge(self.sems["sc"], prev_sov)
                            _ti = nc.tensor.matmul(
                                ps_o[:], self.kv_sb[:, 1, k1 * 128:(k1 + 1) * 128],
                                pt[k1 % 2][:],
                                start=(k1 == 0), stop=(k1 == nkc - 1))
                    mmv = self.inc(_ti, "mm", 1)
                    prev_pv_mm = mmv
                    self.wait(nc.scalar, "mm", mmv)
                    if g >= 2:
                        self.wtok(nc.scalar, otoks[g - 2])
                    ins = nc.scalar.activation(out=ob[p % 2][:], in_=ps_o[:],
                                               func=AF.Identity)
                    sov = self.inc(ins, "sc", 1)
                    prev_sov = sov
                    otoks[g] = asp.issue(
                        lambda q, uu=u, pp=p, b=ob[p % 2]: q.dma_start(
                            out=attn_d[uu * 128:(uu + 1) * 128,
                                       512 * pp:512 * (pp + 1)], in_=b[:]),
                        gate=(self.sems["sc"], sov))

    def p6_wo(self, wo_own, attn_d, rs_in, rs_out):
        nc = self.nc
        st = contextlib.ExitStack()
        with st:
            wo = self.sb("p6_wo", [128, 4, D], BF16, st)
            atb = [self.sb(f"p6_a{i}", [128, 4, 512], BF16, st) for i in range(2)]
            stg = [self.sb(f"p6_s{i}", [128, 512], F32, st) for i in range(2)]
            pm = [self.ps(f"p6_pm{i}", [128, 512], F32, st) for i in range(2)]
            wst = self.stream("Lw", "gpsimd", 3)
            ast = self.stream("Lx", "sync", 3)
            sp = self.stream("Sxa", "gpsimd", 4)
            wo_r = wo_own.rearrange("(ec p) d -> p ec d", p=128)
            at_r = attn_d.rearrange("(ec p) t -> p ec t", p=128)
            rs_w = rs_in.rearrange("(tb dc p) t -> tb p dc t", tb=4, p=128)
            wtok = wst.issue(lambda q: q.dma_start(out=wo[:], in_=wo_r))
            self.wtok(nc.tensor, wtok)
            atoks = {}
            sptk = {}
            scv_of = {}
            gidx = 0
            for tg in range(2):
                atoks[tg] = ast.issue(
                    lambda q, t=tg, b=atb[tg % 2]: q.dma_start(
                        out=b[:], in_=at_r[:, :, t * 512:(t + 1) * 512]))
            for tg in range(16):
                self.wtok(nc.tensor, atoks[tg])
                tb, tcq = divmod(tg, 4)
                for dci in range(DC):
                    pb = pm[gidx % 2]
                    if gidx >= 2:
                        self.wait(nc.tensor, "sc", scv_of[gidx - 2])
                    for ec in range(4):
                        _ti = nc.tensor.matmul(pb[:], wo[:, ec, dci * 128:(dci + 1) * 128],
                                         atb[tg % 2][:, ec, :],
                                         start=(ec == 0), stop=(ec == 3))
                    mmv = self.inc(_ti, "mm", 1)
                    self.wait(nc.scalar, "mm", mmv)
                    if gidx >= 2:
                        self.wtok(nc.scalar, sptk[gidx - 2])
                    ins = nc.scalar.activation(out=stg[gidx % 2][:], in_=pb[:],
                                               func=AF.Identity)
                    scv_of[gidx] = self.inc(ins, "sc", 1)
                    sptk[gidx] = sp.issue(
                        lambda q, t=tb, d=dci, c=tcq, b=stg[gidx % 2]: q.dma_start(
                            out=rs_w[t, :, d, c * 512:(c + 1) * 512], in_=b[:]),
                        gate=(self.sems["sc"], scv_of[gidx]))
                    gidx += 1
                if tg + 2 < 16:
                    atoks[tg + 2] = ast.issue(
                        lambda q, t=tg + 2, b=atb[(tg + 2) % 2]: q.dma_start(
                            out=b[:], in_=at_r[:, :, t * 512:(t + 1) * 512]),
                        gate=(self.sems["mm"], self.cnt["mm"]))
        self.barrier()
        ins = nc.gpsimd.collective_compute(
            "ReduceScatter", ALU.add, replica_groups=GROUPS,
            ins=[rs_in.opt()], outs=[rs_out.opt()])
        self.inc(ins, "cc", 1)
        for eng in (nc.tensor, nc.vector, nc.scalar, nc.sync):
            self.wait(eng, "cc")

    def p6b_x3(self, x2_d, rs_out, x3_d, stats_row):
        nc = self.nc
        st = contextlib.ExitStack()
        with st:
            xt = [self.sb(f"p6b_x{i}", [128, DC, 512], F32, st) for i in range(2)]
            rt = [self.sb(f"p6b_r{i}", [128, DC, 512], F32, st) for i in range(2)]
            xbf = self.sb("p6b_xb", [128, DC, 512], BF16, st)
            sqb = self.sb("p6b_sq", [128, DC, 512], BF16, st)
            pstq = [self.ps(f"p6b_ps{i}", [1, 512], F32, st) for i in range(2)]
            lrow = self.sb("p6b_rows", [1, 4, 512], F32, st)
            xst = self.stream("Lx", "sync", 3)
            rst = self.stream("Lr", "sync", 3)
            sp = self.stream("Sxa", "gpsimd", 4)
            x2_r = x2_d.rearrange("dc p t -> p dc t")
            rs_r = rs_out.rearrange("(dc p) t -> p dc t", p=128)
            xtoks = {}
            rtoks = {}
            ov = {}
            sptoks = {}
            statmm = {}
            rowtoks = {}
            for ci in range(2):
                xtoks[ci] = xst.issue(
                    lambda q, c=ci, b=xt[ci % 2]: q.dma_start(
                        out=b[:], in_=x2_r[:, :, c * 512:(c + 1) * 512]))
                rtoks[ci] = rst.issue(
                    lambda q, c=ci, b=rt[ci % 2]: q.dma_start(
                        out=b[:], in_=rs_r[:, :, c * 512:(c + 1) * 512]))
            for ci in range(4):
                tsl = slice(ci * 512, (ci + 1) * 512)
                self.wtok(nc.vector, xtoks[ci])
                self.wtok(nc.vector, rtoks[ci])
                if ci >= 2:
                    self.wtok(nc.vector, sptoks[ci - 2])
                if ci >= 1:
                    self.wait(nc.vector, "mm", statmm[ci - 1])
                nc.vector.tensor_tensor(xt[ci % 2][:], xt[ci % 2][:], rt[ci % 2][:],
                                        ALU.add)
                ins = nc.vector.tensor_copy(xbf[:], xt[ci % 2][:])
                ov[ci] = self.inc(ins, "ev", 1)
                sptoks[ci] = sp.issue(
                    lambda q, c=ci, b=xt[ci % 2]: q.dma_start(
                        out=x3_d.rearrange("dc p t -> p dc t")[:, :, c * 512:(c + 1) * 512],
                        in_=b[:]),
                    gate=(self.sems["ev"], ov[ci]))
                if ci + 2 < 4:
                    gate = (self.sems["ev"], ov[ci])
                    xtoks[ci + 2] = xst.issue(
                        lambda q, c=ci + 2, b=xt[(ci + 2) % 2]: q.dma_start(
                            out=b[:], in_=x2_r[:, :, c * 512:(c + 1) * 512]),
                        gate=gate)
                    rtoks[ci + 2] = rst.issue(
                        lambda q, c=ci + 2, b=rt[(ci + 2) % 2]: q.dma_start(
                            out=b[:], in_=rs_r[:, :, c * 512:(c + 1) * 512]),
                        gate=gate)
                self.wait(nc.scalar, "ev", ov[ci])
                if ci >= 1:
                    self.wait(nc.scalar, "mm", statmm[ci - 1])
                ins = nc.scalar.activation(out=sqb[:], in_=xt[ci % 2][:],
                                           func=AF.Square)
                sqv = self.inc(ins, "sc", 1)
                nc.tensor.wait_ge(self.sems["sc"], sqv)
                nc.tensor.wait_ge(self.sems["ev"], ov[ci])
                for dc in range(DC):
                    _ti = nc.tensor.matmul(pstq[0][:], self.c_ones_bf[:], xbf[:, dc, :],
                                     start=(dc == 0), stop=(dc == DC - 1))
                for dc in range(DC):
                    _ti = nc.tensor.matmul(pstq[1][:], self.c_ones_bf[:], sqb[:, dc, :],
                                     start=(dc == 0), stop=(dc == DC - 1))
                statmm[ci] = self.inc(_ti, "mm", 1)
                self.wait(nc.vector, "mm", statmm[ci])
                if ci >= 1:
                    self.wtok(nc.vector, rowtoks[ci - 1])
                scv = self.ln_rows("ln", pstq[0][:], pstq[1][:],
                                   lrow[:, 0, :], lrow[:, 1, :],
                                   lrow[:, 2, :], lrow[:, 3, :])
                rowtoks[ci] = sp.issue(
                    lambda q, c=ci: q.dma_start(
                        out=self.lnrows_d[stats_row:stats_row + 2,
                                          c * 512:(c + 1) * 512],
                        in_=lrow[:, 0:2, :]),
                    gate=(self.sems["sc"], scv))

    def p8_out(self, x4_d, out):
        nc = self.nc
        st = contextlib.ExitStack()
        with st:
            xt = [self.sb(f"p8_x{i}", [128, DC, 128], F32, st) for i in range(2)]
            obuf = [self.sb(f"p8_o{i}", [128, D], F32, st) for i in range(2)]
            ptr = [self.ps(f"p8_p{i}", [128, 512], F32, st) for i in range(2)]
            xst = self.stream("Lx", "sync", 3)
            ost = self.stream("Sout", "sync", 2)
            x4_r = x4_d.rearrange("dc p t -> p dc t")
            xtoks = {}
            otoks = {}
            evv = {}
            mmg = {}
            for tg in range(2):
                xtoks[tg] = xst.issue(
                    lambda q, t=tg, b=xt[tg % 2]: q.dma_start(
                        out=b[:], in_=x4_r[:, :, t * 128:(t + 1) * 128]))
            for tg in range(16):
                self.wtok(nc.tensor, xtoks[tg])
                if tg >= 2:
                    self.wtok(nc.tensor, otoks[tg - 2])
                for half in range(4):
                    pb = ptr[half % 2]
                    gl = tg * 4 + half
                    if gl >= 2:
                        self.wait(nc.tensor, "ev", evv[gl - 2])
                    for k in range(4):
                        dc = half * 4 + k
                        _ti = nc.tensor.transpose(pb[:, k * 128:(k + 1) * 128],
                                            xt[tg % 2][:, dc, :], self.c_ident_f[:])
                    mmv = self.inc(_ti, "mm", 1)
                    self.wait(nc.vector, "mm", mmv)
                    ins = nc.vector.tensor_copy(
                        obuf[tg % 2][:, half * 512:(half + 1) * 512], pb[:])
                    evv[gl] = self.inc(ins, "ev", 1)
                mmg[tg] = self.cnt["mm"]
                if tg + 2 < 16:
                    xtoks[tg + 2] = xst.issue(
                        lambda q, t=tg + 2, b=xt[(tg + 2) % 2]: q.dma_start(
                            out=b[:], in_=x4_r[:, :, t * 128:(t + 1) * 128]),
                        gate=(self.sems["mm"], mmg[tg]))
                otoks[tg] = ost.issue(
                    lambda q, t=tg, b=obuf[tg % 2]: q.dma_start(
                        out=out[t * 128:(t + 1) * 128, :], in_=b[:]),
                    gate=(self.sems["ev"], evv[tg * 4 + 3]))


def _prep_inputs(inputs):
    x = np.asarray(inputs["L0_attn_a_x"], dtype=np.float32).reshape(B_ * T_, D)
    w_inT = np.ascontiguousarray(np.asarray(inputs["L0_attn_a_W_in"], dtype=np.float32).T)
    conv_w = np.ascontiguousarray(np.asarray(inputs["L0_attn_a_conv_w"], dtype=np.float32)
                                  .reshape(32, 128, 4).transpose(1, 0, 2))
    conv_b = np.ascontiguousarray(np.asarray(inputs["L0_attn_a_conv_b"], dtype=np.float32)
                                  .reshape(32, 128, 1).transpose(1, 0, 2))
    w_outT = np.ascontiguousarray(np.asarray(inputs["L0_attn_a_W_out"], dtype=np.float32).T)
    mlp1T = np.ascontiguousarray(np.asarray(inputs["L0_attn_a_W_mlp1"], dtype=np.float32).T)
    mlp2T = np.ascontiguousarray(np.asarray(inputs["L0_attn_a_W_mlp2"], dtype=np.float32).T)
    w1T = np.ascontiguousarray(np.asarray(inputs["L0_ffn_W1"], dtype=np.float32).T)
    w3T = np.ascontiguousarray(np.asarray(inputs["L0_ffn_W3"], dtype=np.float32).T)
    w2T = np.ascontiguousarray(np.asarray(inputs["L0_ffn_W2"], dtype=np.float32).T)
    wq = np.asarray(inputs["L0_attn_b_Wq"], dtype=np.float32)
    wk = np.asarray(inputs["L0_attn_b_Wk"], dtype=np.float32)
    wv = np.asarray(inputs["L0_attn_b_Wv"], dtype=np.float32)
    wo = np.asarray(inputs["L0_attn_b_Wo"], dtype=np.float32)
    consts = _host_consts()
    maps = []
    for c in range(8):
        b, g = divmod(c, 4)
        t0 = b * T_ + g * TC
        m = dict(consts)
        m["x_own"] = np.ascontiguousarray(x[t0:t0 + TC])
        xh = np.zeros((128, D), dtype=np.float32)
        if g > 0:
            xh[0:HALO] = x[t0 - HALO:t0]
        m["x_halo"] = xh
        m.update(w_inT=w_inT, conv_w=conv_w, conv_b=conv_b, w_outT=w_outT,
                 mlp1T=mlp1T, mlp2T=mlp2T, w1T=w1T, w3T=w3T, w2T=w2T)
        m["wq_own"] = np.ascontiguousarray(wq[:, 512 * g:512 * (g + 1)])
        m["wk_own"] = np.ascontiguousarray(wk[:, HDIM * g:HDIM * (g + 1)])
        m["wv_own"] = np.ascontiguousarray(wv[:, HDIM * g:HDIM * (g + 1)])
        m["wo_own"] = np.ascontiguousarray(wo[512 * g:512 * (g + 1), :])
        tt = _per_core_tables(g)
        m["slope"] = tt["slope"]
        m["dbias"] = tt["dbias"]
        maps.append(m)
    return maps


def _run(inputs, taps=(), phase_limit=99, trace=False):
    mk = MK(taps=taps)
    with mk.es:
        mk.build(phase_limit=phase_limit)
    maps = _prep_inputs(inputs)
    res = run_bass_kernel_spmd(mk.nc, maps, core_ids=list(range(8)), trace=trace)
    return res, mk.out_names


def kernel(**inputs):
    res, _ = _run(inputs)
    outp = np.empty((B_ * T_, D), dtype=np.float32)
    for c in range(8):
        b, g = divmod(c, 4)
        t0 = b * T_ + g * TC
        outp[t0:t0 + TC] = res.results[c]["out"]
    return outp.reshape(B_, T_, D)


# version 05 (final)
# speedup vs baseline: 1.0550x; 1.0550x over previous
import contextlib
import numpy as np
import ml_dtypes
import concourse.bass as bass
import concourse.mybir as mybir
from concourse.bass_utils import run_bass_kernel_spmd

F32 = mybir.dt.float32
BF16 = mybir.dt.bfloat16
AF = mybir.ActivationFunctionType
ALU = mybir.AluOpType

D = 2048
DC = 16
TC = 2048
B_, T_ = 2, 8192
DI = 4096
DH = 8192
NH, HDIM = 16, 128
HALO = 3
TW = TC + HALO
LNEPS = 1e-5
LOG_RTD = float(np.log(np.sqrt(2048.0)))
INVSQ = 1.0 / float(np.sqrt(128.0))
INVN_H = 1.0 / float(np.sqrt(8192.0))
BIGNEG = -1e30
GROUPS = [[0, 1, 2, 3], [4, 5, 6, 7]]

bf16 = ml_dtypes.bfloat16


def _pm_matrix(n):
    a = np.arange(n)
    m = a[:, None] & a[None, :]
    pc = np.zeros((n, n), dtype=np.int64)
    v = m.copy()
    while v.any():
        pc += v & 1
        v >>= 1
    return np.where(pc % 2 == 0, 1.0, -1.0).astype(np.float32)


def _host_consts():
    c = {}
    c["ident_bf"] = np.eye(128, dtype=bf16)
    c["ident_f"] = np.eye(128, dtype=np.float32)
    c["ones_col_bf"] = np.ones((128, 1), dtype=bf16)
    c["ones_row_f"] = np.ones((1, 128), dtype=np.float32)
    c["h128_bf"] = _pm_matrix(128).astype(bf16)
    c["h64_bf"] = _pm_matrix(64).astype(bf16)
    theta = 1.0 / (10000.0 ** (np.arange(64, dtype=np.float64) / 64))
    fr = np.outer(theta, np.arange(T_, dtype=np.float64))
    c["cos_t"] = np.cos(fr).astype(np.float32)
    c["sin_t"] = np.sin(fr).astype(np.float32)
    jj = np.arange(1024)[None, :]
    pp = np.arange(128)[:, None]
    c["dwide"] = (jj - pp).astype(np.float32)
    mw = np.zeros((4, 128, 1024), dtype=np.float32)
    mn = np.zeros((4, 128, 512), dtype=np.float32)
    for ci in range(4):
        mw[ci] = np.where(jj <= (512 + 128 * ci) + pp, 0.0, BIGNEG)
        mn[ci] = np.where(jj[:, :512] <= (128 * ci) + pp, 0.0, BIGNEG)
    c["mask_w"] = np.ascontiguousarray(mw.transpose(1, 0, 2)).astype(bf16)
    c["mask_n"] = np.ascontiguousarray(mn.transpose(1, 0, 2)).astype(bf16)
    cb = np.zeros((128, 4), dtype=np.float32)
    cb[:, 0] = LNEPS
    cb[:, 1] = LOG_RTD
    c["cbias"] = cb
    return c


_DVALS = (np.arange(-63, 1) * 128).astype(np.int64)
_DMAP = {int(v): i for i, v in enumerate(_DVALS)}


def _per_core_tables(g):
    heads = [4 * g + u for u in range(4)]
    slopes = np.array([2.0 ** (-8.0 * (h + 1) / NH) for h in heads], dtype=np.float64)
    t = {"slope": np.broadcast_to(slopes.astype(np.float32), (128, 4)).copy()}
    db = np.zeros((4, 64), dtype=np.float64)
    for u in range(4):
        db[u] = slopes[u] * _DVALS
    t["dbias"] = np.broadcast_to(db.reshape(1, 256).astype(np.float32), (128, 256)).copy()
    return t


class Stream:

    def __init__(self, mk, name, qname, nslots):
        self.mk = mk
        self.qname = qname
        self.n = nslots
        self.names = [f"{name}{i}" for i in range(nslots)]
        for nm in self.names:
            mk.sem(nm)
        self.i = 0

    def issue(self, fn, gate=None):
        mk = self.mk
        q = getattr(mk.nc, self.qname)
        s = self.i % self.n
        nm = self.names[s]
        if gate is not None:
            for gh, gv in ([gate] if isinstance(gate, tuple) else gate):
                q.wait_ge(gh, gv)
        if mk.cnt[nm] > 0:
            q.wait_ge(mk.sems[nm], mk.cnt[nm])
        ins = fn(q)
        mk.inc(ins, nm, 16)
        self.i += 1
        return (mk.sems[nm], mk.cnt[nm])


class MK:
    def __init__(self, taps=(), detect_races=True):
        self.nc = bass.Bass("TRN2", target_bir_lowering=False,
                            detect_race_conditions=detect_races)
        self.taps = list(taps)
        self.es = contextlib.ExitStack()
        self.sems = {}
        self.cnt = {}
        self.dram = {}
        self.out_names = []
        self.streams = {}
        self.dirty = set()
        self.phase_no = 0
        self.carry_ctx = None

    def sem(self, name):
        h = self.es.enter_context(self.nc.semaphore(name))
        self.sems[name] = h
        self.cnt[name] = 0
        return h

    def inc(self, instr, name, n=1):
        instr.then_inc(self.sems[name], n)
        self.cnt[name] += n
        self.dirty.add(name)
        return self.cnt[name]

    def tok(self, name):
        return (self.sems[name], self.cnt[name])

    def wtok(self, eng, token):
        eng.wait_ge(token[0], token[1])

    def wait(self, eng, name, v=None):
        eng.wait_ge(self.sems[name], self.cnt[name] if v is None else v)

    def stream(self, name, qname, nslots):
        if name not in self.streams:
            self.streams[name] = Stream(self, name, qname, nslots)
        st = self.streams[name]
        assert st.qname == qname
        return st

    def barrier(self, skip=()):
        nc = self.nc
        keep = set()
        for name in sorted(self.dirty):
            if name == "phase":
                continue
            if name in skip:
                keep.add(name)
                continue
            nc.sync.wait_ge(self.sems[name], self.cnt[name])
        self.dirty.clear()
        self.dirty |= keep
        self.phase_no += 1
        self.inc(nc.sync.nop(), "phase", 1)
        for eng in (nc.tensor, nc.vector, nc.scalar, nc.gpsimd):
            eng.wait_ge(self.sems["phase"], self.phase_no)

    def din(self, name, shape, dtype=F32):
        ap = self.nc.dram_tensor(name, list(shape), dtype, kind="ExternalInput").ap()
        self.dram[name] = ap
        return ap

    def dout(self, name, shape, dtype=F32):
        ap = self.nc.dram_tensor(name, list(shape), dtype, kind="ExternalOutput").ap()
        self.dram[name] = ap
        self.out_names.append(name)
        return ap

    def dscratch(self, name, shape, dtype=F32):
        ap = self.nc.dram_tensor(name, list(shape), dtype).ap()
        self.dram[name] = ap
        return ap

    def sb(self, name, shape, dtype=F32, stack=None, side=None):
        return (stack or self.es).enter_context(
            self.nc.sbuf_tensor(name, list(shape), dtype, side=side))

    def ps(self, name, shape, dtype=F32, stack=None):
        return (stack or self.es).enter_context(
            self.nc.psum_tensor(name, list(shape), dtype))

    def carry(self, name, shape, dtype):
        assert self.carry_ctx is None
        self.carry_ctx = self.nc.sbuf_tensor(name, list(shape), dtype, side="right")
        return self.carry_ctx.__enter__()

    def carry_free(self):
        self.carry_ctx.__exit__(None, None, None)
        self.carry_ctx = None

    def bcast(self, psum_ap, row_ap):
        return self.nc.tensor.matmul(psum_ap, self.c_ones_row[:], row_ap,
                                     start=True, stop=True)

    def emit_taps(self):
        nc = self.nc
        for tname in self.taps:
            src = self.dram[tname]
            dst = self.nc.dram_tensor("tap_" + tname, list(src.shape), src.dtype,
                                      kind="ExternalOutput").ap()
            self.out_names.append("tap_" + tname)
            self.inc(nc.sync.dma_start(out=dst, in_=src), "store", 16)

    def build(self, phase_limit=99):
        nc = self.nc
        x_own = self.din("x_own", [TC, D])
        x_halo = self.din("x_halo", [128, D])
        w_inT = self.din("w_inT", [D, 2 * DI])
        conv_w = self.din("conv_w", [128, 32, 4])
        conv_b = self.din("conv_b", [128, 32, 1])
        w_outT = self.din("w_outT", [DI, D])
        mlp1T = self.din("mlp1T", [D, DH])
        mlp2T = self.din("mlp2T", [DH, D])
        w1T = self.din("w1T", [D, DH])
        w3T = self.din("w3T", [D, DH])
        w2T = self.din("w2T", [DH, D])
        wq_own = self.din("wq_own", [D, 512])
        wk_own = self.din("wk_own", [D, HDIM])
        wv_own = self.din("wv_own", [D, HDIM])
        wo_own = self.din("wo_own", [512, D])
        ident_bf = self.din("ident_bf", [128, 128], BF16)
        ident_f = self.din("ident_f", [128, 128], F32)
        ones_col_bf = self.din("ones_col_bf", [128, 1], BF16)
        ones_row_f = self.din("ones_row_f", [1, 128], F32)
        h128_bf = self.din("h128_bf", [128, 128], BF16)
        h64_bf = self.din("h64_bf", [64, 64], BF16)
        cos_t = self.din("cos_t", [64, T_])
        sin_t = self.din("sin_t", [64, T_])
        dwide = self.din("dwide", [128, 1024])
        mask_w = self.din("mask_w", [128, 4, 1024], BF16)
        mask_n = self.din("mask_n", [128, 4, 512], BF16)
        slope = self.din("slope", [128, 4])
        dbias = self.din("dbias", [128, 256])
        cbias = self.din("cbias", [128, 4])

        out = self.dout("out", [TC, D], F32)

        r0_d = self.dscratch("r0", [DC, 128, TC])
        xa0_d = self.dscratch("xa0", [DC, 128, TC])
        y_d = self.dscratch("y", [DI, TC], BF16)
        xa1_d = self.dscratch("xa1", [DC, 128, TC])
        r_d = self.dscratch("rrelu", [DH, TC], BF16)
        x1_d = self.dscratch("x1", [DC, 128, TC])
        hid_d = self.dscratch("hid", [DH, TC], BF16)
        x2_d = self.dscratch("x2", [DC, 128, TC])
        x3_d = self.dscratch("x3", [DC, 128, TC])
        x4_d = self.dscratch("x4", [DC, 128, TC])
        w2y_d = self.dscratch("w2y", [DH, D], BF16)
        w2p_d = self.dscratch("w2p", [DH, D], BF16)
        q_d = self.dscratch("qh", [4, 128, T_], BF16)
        attn_d = self.dscratch("attn", [512, T_], BF16)
        xg_in = self.dscratch("xg_in", [4, D, 512], BF16)
        xg_out = self.dscratch("xg_out", [4, 4 * D, 512], BF16)
        rs_in = self.dscratch("rs_in", [4, 4 * D, 512], BF16)
        rs_out = self.dscratch("rs_out", [4, D, 512], BF16)

        for s in ["mm", "ev", "sc", "gp", "cc", "phase", "store", "ldc"]:
            self.sem(s)

        self.c_ident_bf = self.sb("c_ident_bf", [128, 128], BF16, side="right")
        self.c_ident_f = self.sb("c_ident_f", [128, 128], F32, side="right")
        self.c_ones_bf = self.sb("c_ones_bf", [128, 1], BF16, side="right")
        self.c_ones_row = self.sb("c_ones_row", [1, 128], F32, side="right")
        self.c_bias = self.sb("c_bias", [128, 4], F32, side="right")
        self.c_wtap = self.sb("c_wtap", [128, 32, 4], F32, side="right")
        self.c_wbias = self.sb("c_wbias", [128, 32, 1], F32, side="right")
        self.c_h128 = self.sb("c_h128", [128, 128], BF16, side="right")
        self.c_h64 = self.sb("c_h64", [64, 64], BF16, side="right")
        self.c_dwide = self.sb("c_dwide", [128, 1024], F32, side="right")
        self.c_maskw = self.sb("c_maskw", [128, 4, 1024], BF16, side="right")
        self.c_maskn = self.sb("c_maskn", [128, 4, 512], BF16, side="right")
        self.c_slope = self.sb("c_slope", [128, 4], F32, side="right")
        self.c_dbias = self.sb("c_dbias", [128, 256], F32, side="right")
        self.lnrows_d = self.dscratch("lnrows", [6, TC])

        for dst, src in [(self.c_ident_bf, ident_bf), (self.c_ident_f, ident_f),
                         (self.c_ones_bf, ones_col_bf), (self.c_ones_row, ones_row_f),
                         (self.c_bias, cbias),
                         (self.c_wtap, conv_w), (self.c_wbias, conv_b),
                         (self.c_h128, h128_bf), (self.c_h64, h64_bf),
                         (self.c_dwide, dwide), (self.c_maskw, mask_w),
                         (self.c_maskn, mask_n), (self.c_slope, slope),
                         (self.c_dbias, dbias)]:
            self.inc(nc.sync.dma_start(out=dst[:], in_=src), "ldc", 16)
        for eng in (nc.tensor, nc.vector, nc.scalar, nc.gpsimd):
            self.wait(eng, "ldc")

        self.p1a(x_own, x_halo, r0_d, xa0_d)
        self.barrier()
        if phase_limit >= 2:
            self.p1b(w_inT, y_d)
            self.barrier()
        if phase_limit >= 3:
            h2_d = self.dscratch("h2", [DC, 128, TC], BF16)
            self.p1c_a(w_outT, y_d, xa0_d, xa1_d, h2_d)
            self.p1c_b(mlp1T, h2_d, r_d)
            self.barrier()
            self.contract8k("mlp2", mlp2T, False, r_d, xa1_d, r0_d, x1_d,
                            stats_row=0)
            self.barrier()
        if phase_limit >= 4:
            self.fold_w2(w2T, w2y_d, w2p_d)
            self.barrier()
            self.ffn("f1", x1_d, w1T, w3T, w2p_d, hid_d, x2_d,
                     ln_row_in=0, stats_row=2)
            self.barrier()
        if phase_limit >= 5:
            self.p3_gather(x2_d, xg_in, xg_out, ln_row=2)
        if phase_limit >= 6:
            self.p4_qkv(xg_out, wq_own, wk_own, wv_own, cos_t, sin_t, q_d)
            self.barrier()
            self.p5_attn(q_d, attn_d)
            self.barrier()
            self.carry_free()
        if phase_limit >= 7:
            self.p6_wo(wo_own, attn_d, rs_in, rs_out)
            self.p6b_x3(x2_d, rs_out, x3_d, stats_row=4)
            self.barrier()
            self.ffn("f2", x3_d, w1T, w3T, w2p_d, hid_d, x4_d,
                     ln_row_in=4, stats_row=None)
            self.barrier()
        if phase_limit >= 8:
            self.p8_out(x4_d, out)
            self.barrier()

        self.emit_taps()
        for name in sorted(set(list(self.dirty) + ["store"])):
            if name != "phase":
                self.wait(nc.gpsimd, name)
        if self.carry_ctx is not None:
            self.carry_free()
        return self

    def ln_rows(self, kind, psum_sum, psum_sq, m_ap, r_ap, s1, s2):
        nc = self.nc
        if kind == "ln":
            nc.vector.tensor_scalar(m_ap, psum_sum, 1.0 / D, None, ALU.mult)
            nc.vector.tensor_scalar(s1, psum_sq, 1.0 / D, None, ALU.mult)
            nc.vector.tensor_tensor(s2, m_ap, m_ap, ALU.mult)
            ins = nc.vector.tensor_tensor(s1, s1, s2, ALU.subtract)
            vv = self.inc(ins, "ev", 1)
            self.wait(nc.scalar, "ev", vv)
            nc.scalar.activation(out=s2, in_=s1, func=AF.Ln,
                                 bias=self.c_bias[0:1, 0:1])
            ins = nc.scalar.activation(out=r_ap, in_=s2, func=AF.Exp, scale=-0.5)
            return self.inc(ins, "sc", 1)
        else:
            ins = nc.vector.tensor_scalar(s1, psum_sq, 1e-24, None, ALU.max)
            vv = self.inc(ins, "ev", 1)
            self.wait(nc.scalar, "ev", vv)
            nc.scalar.activation(out=s2, in_=s1, func=AF.Ln)
            ins = nc.scalar.activation(out=r_ap, in_=s2, func=AF.Exp,
                                       scale=-0.5, bias=self.c_bias[0:1, 1:2])
            return self.inc(ins, "sc", 1)

    def p1a(self, x_own, x_halo, r0_d, xa0_d):
        nc = self.nc
        st = contextlib.ExitStack()
        self.h_sb = self.carry("h", [128, DC, TW], BF16)
        with st:
            xt = [self.sb(f"p1a_xt{i}", [128, D], F32, st) for i in range(2)]
            xs = self.sb("p1a_xs", [128, DC, 128], F32, st)
            xs_bf = self.sb("p1a_xsbf", [128, DC, 128], BF16, st)
            sq_bf = self.sb("p1a_sqbf", [128, DC, 128], BF16, st)
            tmp = self.sb("p1a_tmp", [128, DC, 128], F32, st)
            xa0s = self.sb("p1a_xa0", [128, DC, 128], F32, st)
            ptr = [self.ps(f"p1a_ptr{i}", [128, 512], F32, st) for i in range(2)]
            pst = [self.ps(f"p1a_pst{i}", [1, 128], F32, st) for i in range(2)]
            pbc = self.ps("p1a_pbc", [128, 256], F32, st)
            lrow = self.sb("p1a_rows", [1, 5, 128], F32, st)
            ld = self.stream("Lxt", "sync", 2)
            sp = self.stream("Sxa", "gpsimd", 4)

            mm_after = {}
            ev_after = {}
            sp_r0 = {}
            sp_xa = {}
            for gi in range(17):
                buf = xt[gi % 2]
                gate = (self.sems["mm"], mm_after[gi - 2]) if gi >= 2 else None
                if gi == 0:
                    t = ld.issue(lambda q: q.dma_start(out=buf[:], in_=x_halo),
                                 gate=gate)
                else:
                    t = ld.issue(lambda q, g=gi: q.dma_start(
                        out=buf[:], in_=x_own[(g - 1) * 128: g * 128, :]), gate=gate)

                self.wtok(nc.tensor, t)
                if gi >= 1:
                    self.wait(nc.tensor, "sc")
                for half in range(4):
                    pb = ptr[half % 2]
                    if gi * 4 + half >= 2:
                        self.wait(nc.tensor, "ev", ev_after[gi * 4 + half - 2])
                    for k in range(4):
                        dc = half * 4 + k
                        _ti = nc.tensor.transpose(pb[:, k * 128:(k + 1) * 128],
                                            buf[:, dc * 128:(dc + 1) * 128],
                                            self.c_ident_f[:])
                    mmv = self.inc(_ti, "mm", 1)
                    self.wait(nc.vector, "mm", mmv)
                    if gi >= 2 and half == 0:
                        self.wtok(nc.vector, sp_r0[gi - 1])
                    ins = nc.vector.tensor_copy(xs[:, half * 4:(half + 1) * 4, :], pb[:])
                    ev_after[gi * 4 + half] = self.inc(ins, "ev", 1)
                mm_after[gi] = self.cnt["mm"]

                self.wait(nc.scalar, "ev")
                ins = nc.scalar.activation(out=sq_bf[:], in_=xs[:], func=AF.Square)
                sqv = self.inc(ins, "sc", 1)
                ins = nc.vector.tensor_copy(xs_bf[:], xs[:])
                xbv = self.inc(ins, "ev", 1)

                nc.tensor.wait_ge(self.sems["ev"], xbv)
                nc.tensor.wait_ge(self.sems["sc"], sqv)
                for dc in range(DC):
                    _ti = nc.tensor.matmul(pst[0][:], self.c_ones_bf[:], xs_bf[:, dc, :],
                                     start=(dc == 0), stop=(dc == DC - 1))
                for dc in range(DC):
                    _ti = nc.tensor.matmul(pst[1][:], self.c_ones_bf[:], sq_bf[:, dc, :],
                                     start=(dc == 0), stop=(dc == DC - 1))
                mmv = self.inc(_ti, "mm", 1)
                self.wait(nc.vector, "mm", mmv)
                scv = self.ln_rows("ln", pst[0][:], pst[1][:],
                                   lrow[:, 0, :], lrow[:, 1, :],
                                   lrow[:, 3, :], lrow[:, 4, :])

                nc.tensor.wait_ge(self.sems["sc"], scv)
                _ti = self.bcast(pbc[:, 0:128], lrow[:, 0, :])
                _ti = self.bcast(pbc[:, 128:256], lrow[:, 1, :])
                mmv = self.inc(_ti, "mm", 1)
                self.wait(nc.vector, "mm", mmv)
                if gi >= 2:
                    self.wtok(nc.vector, sp_xa[gi - 1])
                mb = pbc[:, 0:128].unsqueeze(1).broadcast_to([128, DC, 128])
                rb = pbc[:, 128:256].unsqueeze(1).broadcast_to([128, DC, 128])
                nc.vector.tensor_tensor(tmp[:], xs[:], mb, ALU.subtract)
                ins = nc.vector.tensor_tensor(xa0s[:], tmp[:], rb, ALU.mult)
                xav = self.inc(ins, "ev", 1)

                self.wait(nc.scalar, "ev", xav)
                ins = nc.scalar.activation(out=sq_bf[:], in_=xa0s[:], func=AF.Square)
                sqv = self.inc(ins, "sc", 1)
                nc.tensor.wait_ge(self.sems["sc"], sqv)
                for dc in range(DC):
                    _ti = nc.tensor.matmul(pst[0][:], self.c_ones_bf[:], sq_bf[:, dc, :],
                                     start=(dc == 0), stop=(dc == DC - 1))
                mmv = self.inc(_ti, "mm", 1)
                self.wait(nc.vector, "mm", mmv)
                scv = self.ln_rows("rms", None, pst[0][:],
                                   None, lrow[:, 2, :],
                                   lrow[:, 3, :], lrow[:, 4, :])
                nc.tensor.wait_ge(self.sems["sc"], scv)
                _ti = self.bcast(pbc[:, 0:128], lrow[:, 2, :])
                mmv = self.inc(_ti, "mm", 1)
                self.wait(nc.vector, "mm", mmv)
                sb_ = pbc[:, 0:128].unsqueeze(1).broadcast_to([128, DC, 128])
                if gi == 0:
                    ins = nc.vector.tensor_tensor(
                        self.h_sb[:, :, 0:HALO], xa0s[:, :, 0:HALO],
                        pbc[:, 0:HALO].unsqueeze(1).broadcast_to([128, DC, HALO]),
                        ALU.mult)
                else:
                    ins = nc.vector.tensor_tensor(
                        self.h_sb[:, :, HALO + (gi - 1) * 128: HALO + gi * 128],
                        xa0s[:], sb_, ALU.mult)
                hv = self.inc(ins, "ev", 1)

                if gi >= 1:
                    tsl = slice((gi - 1) * 128, gi * 128)
                    sp_r0[gi] = sp.issue(
                        lambda q, s=tsl: q.dma_start(
                            out=r0_d.rearrange("dc p t -> p dc t")[:, :, s], in_=xs[:]),
                        gate=(self.sems["ev"], ev_after[gi * 4 + 3]))
                    sp_xa[gi] = sp.issue(
                        lambda q, s=tsl: q.dma_start(
                            out=xa0_d.rearrange("dc p t -> p dc t")[:, :, s], in_=xa0s[:]),
                        gate=(self.sems["ev"], xav))
                nc.tensor.wait_ge(self.sems["ev"], hv)

            if "h_dump" in self.taps:
                h_d = self.dscratch("h_dump", [128, DC, TW], BF16)
                sp.issue(lambda q: q.dma_start(out=h_d, in_=self.h_sb[:]),
                         gate=self.tok("ev"))

    def p1b(self, w_inT, y_d):
        nc = self.nc
        st = contextlib.ExitStack()
        tcs = [(0, 512), (512, 512), (1024, 512), (1536, 512), (2048, 3)]
        PW = 256
        with st:
            wps = [self.sb(f"p1b_w{i}", [128, DC, PW], BF16, st) for i in range(2)]
            xp = self.sb("p1b_xp", [128, 2, TW], F32, st)
            sz = self.sb("p1b_sz", [128, 2, TW], F32, st)
            xc = self.sb("p1b_xc", [128, TC], F32, st)
            ctmp = self.sb("p1b_ctmp", [128, TC], F32, st)
            sxc = self.sb("p1b_sxc", [128, TC], F32, st)
            yb = [self.sb(f"p1b_yb{i}", [128, TC], BF16, st) for i in range(2)]
            pm = [self.ps(f"p1b_pm{i}", [128, 512], F32, st) for i in range(2)]
            wst = self.stream("Lw", "gpsimd", 3)
            yst = self.stream("Sy", "sync", 4)

            wtoks = {}
            mm_idx = {}
            ytoks = {}
            scv_of = {}
            w_r = w_inT.rearrange("(dc p) f -> p dc f", p=128)
            y_w = y_d.rearrange("(cc p) t -> p cc t", p=128)
            NPE = DI // PW

            def wload(idx):
                pe, part = divmod(idx, 2)
                col0 = part * DI + pe * PW
                gate = (self.sems["mm"], mm_idx[idx - 2]) if idx >= 2 else None
                wtoks[idx] = wst.issue(
                    lambda q, c0=col0, b=wps[idx % 2]: q.dma_start(
                        out=b[:], in_=w_r[:, :, c0:c0 + PW]), gate=gate)

            wload(0)
            wload(1)
            gidx = 0
            convv = {}
            for pe in range(NPE):
                for part in range(2):
                    idx = pe * 2 + part
                    wbuf = wps[idx % 2]
                    dst = xp if part == 0 else sz
                    self.wtok(nc.tensor, wtoks[idx])
                    if pe >= 1 and part == 0:
                        self.wait(nc.tensor, "ev", convv[pe - 1])
                    for m in range(2):
                        for (toff, twid) in tcs:
                            pb = pm[gidx % 2]
                            if gidx >= 2:
                                self.wait(nc.tensor, "sc", scv_of[gidx - 2])
                            for dc in range(DC):
                                _ti = nc.tensor.matmul(
                                    pb[:, 0:twid],
                                    wbuf[:, dc, m * 128:(m + 1) * 128],
                                    self.h_sb[:, dc, toff:toff + twid],
                                    start=(dc == 0), stop=(dc == DC - 1))
                            mmv = self.inc(_ti, "mm", 1)
                            self.wait(nc.scalar, "mm", mmv)
                            fn = AF.Identity if part == 0 else AF.Silu
                            ins = nc.scalar.activation(
                                out=dst[:, m, toff:toff + twid], in_=pb[:, 0:twid],
                                func=fn)
                            scv_of[gidx] = self.inc(ins, "sc", 1)
                            gidx += 1
                    mm_idx[idx] = self.cnt["mm"]
                    if idx + 2 < 2 * NPE:
                        wload(idx + 2)

                for m in range(2):
                    cc = pe * 2 + m
                    self.wait(nc.vector, "sc")
                    nc.vector.tensor_scalar(
                        xc[:], xp[:, m, 3:3 + TC], self.c_wtap[:, cc, 3:4],
                        self.c_wbias[:, cc, 0:1], ALU.mult, ALU.add)
                    for j in (2, 1, 0):
                        nc.vector.tensor_scalar(
                            ctmp[:], xp[:, m, j:j + TC], self.c_wtap[:, cc, j:j + 1],
                            None, ALU.mult)
                        ins = nc.vector.tensor_tensor(xc[:], xc[:], ctmp[:], ALU.add)
                    cv = self.inc(ins, "ev", 1)
                    self.wait(nc.scalar, "ev", cv)
                    ins = nc.scalar.activation(out=sxc[:], in_=xc[:], func=AF.Silu)
                    sv = self.inc(ins, "sc", 1)
                    self.wait(nc.vector, "sc", sv)
                    ybuf = yb[cc % 2]
                    if cc >= 2:
                        self.wtok(nc.vector, ytoks[cc - 2])
                    ins = nc.vector.tensor_tensor(ybuf[:], sxc[:], sz[:, m, 3:3 + TC],
                                                  ALU.mult)
                    yv = self.inc(ins, "ev", 1)
                    if m == 1:
                        convv[pe] = yv
                    ytoks[cc] = yst.issue(
                        lambda q, c=cc, b=ybuf: q.dma_start(
                            out=y_w[:, c, :], in_=b[:]),
                        gate=(self.sems["ev"], yv))

    def p1c_a(self, w_outT, y_d, xa0_d, xa1_d, h2_d):
        nc = self.nc
        self.carry_free()
        st = contextlib.ExitStack()
        with st:
            yh = self.sb("p1cA_y", [128, 32, 1024], BF16, st)
            wks = [self.sb(f"p1cA_w{i}", [128, 32, 128], BF16, st) for i in range(2)]
            xa0t = [self.sb(f"p1cA_x{i}", [128, 1024], F32, st) for i in range(3)]
            xa1t = [self.sb(f"p1cA_o{i}", [128, 1024], F32, st) for i in range(2)]
            sqb = self.sb("p1cA_sq", [128, 1024], BF16, st)
            rld = [self.sb(f"p1cA_r{i}", [128, 1024], F32, st) for i in range(3)]
            h2t = [self.sb(f"p1cA_h2{i}", [128, 1024], BF16, st) for i in range(2)]
            lrow = self.sb("p1cA_rows", [1, 3, 1024], F32, st)
            pm = [self.ps(f"p1cA_pm{i}", [128, 512], F32, st) for i in range(2)]
            pstq = [self.ps(f"p1cA_ps{i}", [1, 512], F32, st) for i in range(2)]
            pbc = self.ps("p1cA_pbc", [128, 1024], F32, st)
            yld = self.stream("Lyh", "sync", 2)
            wst = self.stream("Lw", "gpsimd", 3)
            xst = self.stream("Lx", "sync", 3)
            sp = self.stream("Sxa", "gpsimd", 4)
            rst = self.stream("Lr", "sync", 3)
            wout_r = w_outT.rearrange("(cc p) d -> p cc d", p=128)
            y_r = y_d.rearrange("(cc p) t -> p cc t", p=128)

            for hf in range(2):
                hsl = slice(hf * 1024, (hf + 1) * 1024)
                ytok = yld.issue(
                    lambda q: q.dma_start(out=yh[:], in_=y_r[:, :, hsl]),
                    gate=(self.sems["mm"], self.cnt["mm"]))
                wtoks = {}
                xtoks = {}
                wmm = {}
                xa1v = {}
                sptoks = {}

                def wload(dc, hf=hf):
                    gate = (self.sems["mm"],
                            wmm[dc - 2] if dc >= 2 else self.cnt["mm"])
                    wtoks[dc] = wst.issue(
                        lambda q, d=dc, b=wks[dc % 2]: q.dma_start(
                            out=b[:], in_=wout_r[:, :, d * 128:(d + 1) * 128]),
                        gate=gate)

                def xload(dc, hf=hf, hsl=hsl):
                    gate = (self.sems["ev"],
                            xa1v[dc - 3] if dc >= 3 else self.cnt["ev"])
                    xtoks[dc] = xst.issue(
                        lambda q, d=dc, b=xa0t[dc % 3]: q.dma_start(
                            out=b[:], in_=xa0_d[d, :, hsl]), gate=gate)

                for i in range(2):
                    wload(i)
                for i in range(3):
                    xload(i)
                for dc in range(DC):
                    self.wtok(nc.tensor, wtoks[dc])
                    self.wtok(nc.tensor, ytok)
                    for tq in range(2):
                        pb = pm[tq]
                        if dc >= 1:
                            self.wait(nc.tensor, "ev", xa1v[dc - 1])
                        for cc in range(32):
                            _ti = nc.tensor.matmul(pb[:], wks[dc % 2][:, cc, :],
                                             yh[:, cc, tq * 512:(tq + 1) * 512],
                                             start=(cc == 0), stop=(cc == 31))
                        mmv = self.inc(_ti, "mm", 1)
                        self.wait(nc.vector, "mm", mmv)
                        self.wtok(nc.vector, xtoks[dc])
                        if dc >= 2:
                            self.wtok(nc.vector, sptoks[dc - 2])
                        ins = nc.vector.tensor_tensor(
                            xa1t[dc % 2][:, tq * 512:(tq + 1) * 512], pb[:],
                            xa0t[dc % 3][:, tq * 512:(tq + 1) * 512], ALU.add)
                        xa1v[dc] = self.inc(ins, "ev", 1)
                    sptoks[dc] = sp.issue(
                        lambda q, d=dc, b=xa1t[dc % 2], s=hsl: q.dma_start(
                            out=xa1_d[d, :, s], in_=b[:]),
                        gate=(self.sems["ev"], xa1v[dc]))
                    self.wait(nc.scalar, "ev", xa1v[dc])
                    if dc >= 1:
                        self.wait(nc.scalar, "mm")
                    ins = nc.scalar.activation(out=sqb[:], in_=xa1t[dc % 2][:],
                                               func=AF.Square)
                    sqv = self.inc(ins, "sc", 1)
                    nc.tensor.wait_ge(self.sems["sc"], sqv)
                    for tq in range(2):
                        _ti = nc.tensor.matmul(pstq[tq][:], self.c_ones_bf[:],
                                         sqb[:, tq * 512:(tq + 1) * 512],
                                         start=(dc == 0), stop=(dc == DC - 1))
                    wmm[dc] = self.inc(_ti, "mm", 1)
                    if dc + 2 < DC:
                        wload(dc + 2)
                    if dc + 3 < DC:
                        xload(dc + 3)

                self.wait(nc.vector, "mm")
                self.ln_rows("rms", None, pstq[0][:], None, lrow[:, 0, 0:512],
                             lrow[:, 1, 0:512], lrow[:, 2, 0:512])
                scv = self.ln_rows("rms", None, pstq[1][:], None, lrow[:, 0, 512:1024],
                                   lrow[:, 1, 512:1024], lrow[:, 2, 512:1024])
                self.wait(nc.tensor, "ev")
                nc.tensor.wait_ge(self.sems["sc"], scv)
                _ti = self.bcast(pbc[:, 0:512], lrow[:, 0, 0:512])
                _ti = self.bcast(pbc[:, 512:1024], lrow[:, 0, 512:1024])
                mmv = self.inc(_ti, "mm", 1)
                rtoks = {}
                h2v = {}

                def rload(dc, hsl=hsl):
                    gate = (self.sems["ev"],
                            h2v[dc - 3] if dc >= 3 else self.cnt["ev"])
                    g2 = sptoks[dc]
                    gate = [gate, g2]
                    rtoks[dc] = rst.issue(
                        lambda q, d=dc, b=rld[dc % 3]: q.dma_start(
                            out=b[:], in_=xa1_d[d, :, hsl]), gate=gate)

                for i in range(3):
                    rload(i)
                h2toks = {}
                for dc in range(DC):
                    self.wait(nc.vector, "mm", mmv)
                    self.wtok(nc.vector, rtoks[dc])
                    if dc >= 2:
                        self.wtok(nc.vector, h2toks[dc - 2])
                    ins = nc.vector.tensor_tensor(
                        h2t[dc % 2][:], rld[dc % 3][:], pbc[:], ALU.mult)
                    h2v[dc] = self.inc(ins, "ev", 1)
                    h2toks[dc] = sp.issue(
                        lambda q, d=dc, b=h2t[dc % 2], s=hsl: q.dma_start(
                            out=h2_d[d, :, s], in_=b[:]),
                        gate=(self.sems["ev"], h2v[dc]))
                    if dc + 3 < DC:
                        rload(dc + 3)
                self.barrier()

    def p1c_b(self, mlp1T, h2_d, r_d):
        nc = self.nc
        st = contextlib.ExitStack()
        with st:
            h2l = self.sb("p1cB_h2", [128, DC, TC], BF16, st)
            wks = [self.sb(f"p1cB_w{i}", [128, DC, 128], BF16, st) for i in range(3)]
            rb = [self.sb(f"p1cB_r{i}", [128, TC], BF16, st) for i in range(2)]
            pm = [[self.ps(f"p1cB_pm{s}_{i}", [128, 512], F32, st) for i in range(4)]
                  for s in range(2)]
            wst = self.stream("Lw", "gpsimd", 3)
            rstr = self.stream("Sy", "sync", 4)
            h2ld = self.stream("Lyh", "sync", 2)
            w_r = mlp1T.rearrange("(dc p) f -> p dc f", p=128)
            r_w = r_d.rearrange("(m p) t -> p m t", p=128)
            h2tok = h2ld.issue(lambda q: q.dma_start(
                out=h2l[:], in_=h2_d.rearrange("dc p t -> p dc t")))
            self.wtok(nc.tensor, h2tok)

            wtoks = {}
            wmm = {}
            rtoks = {}
            scv_m = {}

            def wload(m):
                gate = (self.sems["mm"],
                        wmm[m - 3] if m >= 3 else self.cnt["mm"])
                wtoks[m] = wst.issue(
                    lambda q, mm_=m, b=wks[m % 3]: q.dma_start(
                        out=b[:], in_=w_r[:, :, mm_ * 128:(mm_ + 1) * 128]), gate=gate)

            for m in range(3):
                wload(m)
            for m in range(64):
                self.wtok(nc.tensor, wtoks[m])
                ps4 = pm[m % 2]
                if m >= 2:
                    self.wait(nc.tensor, "sc", scv_m[m - 2])
                for dc in range(DC):
                    for ci in range(4):
                        _ti = nc.tensor.matmul(ps4[ci][:], wks[m % 3][:, dc, :],
                                               h2l[:, dc, ci * 512:(ci + 1) * 512],
                                               start=(dc == 0), stop=(dc == DC - 1))
                mmv = self.inc(_ti, "mm", 1)
                self.wait(nc.scalar, "mm", mmv)
                if m >= 2:
                    self.wtok(nc.scalar, rtoks[m - 2])
                for ci in range(4):
                    ins = nc.scalar.activation(
                        out=rb[m % 2][:, ci * 512:(ci + 1) * 512], in_=ps4[ci][:],
                        func=AF.Relu)
                scv_m[m] = self.inc(ins, "sc", 1)
                wmm[m] = self.cnt["mm"]
                if m + 3 < 64:
                    wload(m + 3)
                rtoks[m] = rstr.issue(
                    lambda q, mm_=m, b=rb[m % 2]: q.dma_start(
                        out=r_w[:, mm_, :], in_=b[:]),
                    gate=(self.sems["sc"], scv_m[m]))

    def contract8k(self, tag, w_dram, w_is_bf16, act_d, add1_d, add2_d, out_d,
                   stats_row=None):
        nc = self.nc
        st = contextlib.ExitStack()
        mstep = 256 if w_is_bf16 else 128
        nmo = D // mstep
        KC = DH // 128
        with st:
            ah = self.sb(f"c8{tag}_a", [128, KC, 512], BF16, st)
            wks = [self.sb(f"c8{tag}_w{i}", [128, KC, mstep], BF16, st)
                   for i in range(2)]
            a1t = [self.sb(f"c8{tag}_a1{i}", [128, 512], F32, st) for i in range(3)]
            a2t = ([self.sb(f"c8{tag}_a2{i}", [128, 512], F32, st) for i in range(3)]
                   if add2_d is not None else None)
            ot = [self.sb(f"c8{tag}_o{i}", [128, 512], F32, st) for i in range(2)]
            sqb = self.sb(f"c8{tag}_sq", [128, 512], BF16, st)
            xbf = self.sb(f"c8{tag}_xb", [128, 512], BF16, st)
            lrow = self.sb(f"c8{tag}_rows", [1, 4, 512], F32, st)
            pm = [self.ps(f"c8{tag}_pm{i}", [128, 512], F32, st) for i in range(2)]
            pstq = [self.ps(f"c8{tag}_ps{i}", [1, 512], F32, st) for i in range(2)]
            ald = self.stream("Lyh", "sync", 2)
            wst = self.stream("Lw", "gpsimd", 3)
            xst = self.stream("Lx", "sync", 3)
            x2st = self.stream("Lr", "sync", 3) if add2_d is not None else None
            sp = self.stream("Sxa", "gpsimd", 4)
            act_r = act_d.rearrange("(cc p) t -> p cc t", p=128)
            w_rr = w_dram.rearrange("(cc p) d -> p cc d", p=128)

            for qt in range(4):
                qsl = slice(qt * 512, (qt + 1) * 512)
                atok = ald.issue(
                    lambda q: q.dma_start(out=ah[:], in_=act_r[:, :, qsl]),
                    gate=(self.sems["mm"], self.cnt["mm"]))
                wtoks = {}
                wmm = {}
                evv = {}
                a1toks = {}
                a2toks = {}
                sptoks = {}
                statv = {}

                def wload(mo):
                    gate = (self.sems["mm"],
                            wmm[mo - 2] if mo >= 2 else self.cnt["mm"])
                    wtoks[mo] = wst.issue(
                        lambda q, m=mo, b=wks[mo % 2]: q.dma_start(
                            out=b[:], in_=w_rr[:, :, m * mstep:(m + 1) * mstep]),
                        gate=gate)

                def aload(dc):
                    gate = (self.sems["ev"],
                            evv[dc - 3] if dc >= 3 else self.cnt["ev"])
                    a1toks[dc] = xst.issue(
                        lambda q, d=dc, b=a1t[dc % 3]: q.dma_start(
                            out=b[:], in_=add1_d[d, :, qsl]), gate=gate)
                    if add2_d is not None:
                        a2toks[dc] = x2st.issue(
                            lambda q, d=dc, b=a2t[dc % 3]: q.dma_start(
                                out=b[:], in_=add2_d[d, :, qsl]), gate=gate)

                wload(0)
                wload(1)
                for i in range(3):
                    aload(i)
                for mo in range(nmo):
                    self.wtok(nc.tensor, wtoks[mo])
                    self.wtok(nc.tensor, atok)
                    for sub in range(mstep // 128):
                        dc = mo * (mstep // 128) + sub
                        pb = pm[dc % 2]
                        if dc >= 2:
                            self.wait(nc.tensor, "ev", evv[dc - 2])
                        for cc in range(KC):
                            _ti = nc.tensor.matmul(
                                pb[:], wks[mo % 2][:, cc, sub * 128:(sub + 1) * 128],
                                ah[:, cc, :], start=(cc == 0), stop=(cc == KC - 1))
                        mmv = self.inc(_ti, "mm", 1)
                        self.wait(nc.vector, "mm", mmv)
                        self.wtok(nc.vector, a1toks[dc])
                        obuf = ot[dc % 2]
                        if dc >= 2:
                            self.wtok(nc.vector, sptoks[dc - 2])
                        if dc >= 1 and stats_row is not None:
                            self.wait(nc.vector, "mm", statv[dc - 1])
                        ins = nc.vector.tensor_tensor(obuf[:], pb[:], a1t[dc % 3][:],
                                                      ALU.add)
                        if add2_d is not None:
                            self.wtok(nc.vector, a2toks[dc])
                            ins = nc.vector.tensor_tensor(obuf[:], obuf[:],
                                                          a2t[dc % 3][:], ALU.add)
                        if stats_row is not None:
                            ins = nc.vector.tensor_copy(xbf[:], obuf[:])
                        evv[dc] = self.inc(ins, "ev", 1)
                        sptoks[dc] = sp.issue(
                            lambda q, d=dc, b=ot[dc % 2]: q.dma_start(
                                out=out_d[d, :, qsl], in_=b[:]),
                            gate=(self.sems["ev"], evv[dc]))
                        if dc + 3 < DC:
                            aload(dc + 3)
                        if stats_row is not None:
                            self.wait(nc.scalar, "ev", evv[dc])
                            if dc >= 1:
                                self.wait(nc.scalar, "mm", statv[dc - 1])
                            ins = nc.scalar.activation(out=sqb[:], in_=obuf[:],
                                                       func=AF.Square)
                            sqv = self.inc(ins, "sc", 1)
                            nc.tensor.wait_ge(self.sems["sc"], sqv)
                            nc.tensor.wait_ge(self.sems["ev"], evv[dc])
                            _ti = nc.tensor.matmul(
                                pstq[0][:], self.c_ones_bf[:], xbf[:],
                                start=(dc == 0), stop=(dc == DC - 1))
                            _ti = nc.tensor.matmul(
                                pstq[1][:], self.c_ones_bf[:], sqb[:],
                                start=(dc == 0), stop=(dc == DC - 1))
                            statv[dc] = self.inc(_ti, "mm", 1)
                    wmm[mo] = self.cnt["mm"]
                    if mo + 2 < nmo:
                        wload(mo + 2)
                if stats_row is not None:
                    self.wait(nc.vector, "mm")
                    if qt >= 1:
                        self.wtok(nc.vector, self._c8_rowtok)
                    scv = self.ln_rows("ln", pstq[0][:], pstq[1][:],
                                       lrow[:, 0, :], lrow[:, 1, :],
                                       lrow[:, 2, :], lrow[:, 3, :])
                    self._c8_rowtok = sp.issue(
                        lambda q: q.dma_start(
                            out=self.lnrows_d[stats_row:stats_row + 2, qsl],
                            in_=lrow[:, 0:2, :]),
                        gate=(self.sems["sc"], scv))

    def fold_w2(self, w2T, w2y_d, w2p_d):
        nc = self.nc
        st = contextlib.ExitStack()
        with st:
            wt = [self.sb(f"fw_w{i}", [128, D], BF16, st) for i in range(2)]
            ev = [self.sb(f"fw_e{i}", [128, D], BF16, st) for i in range(2)]
            st1 = contextlib.ExitStack()
            st1.__enter__()
            pmm = [self.ps(f"fw_p{i}", [128, 1024], F32, st1) for i in range(2)]
            wst = self.stream("Lw", "gpsimd", 3)
            ost = self.stream("Sy", "sync", 4)
            evv = {}
            otoks = {}
            mmc = {}
            for c in range(64):
                gate = (self.sems["mm"], mmc[c - 2]) if c >= 2 else None
                wtok = wst.issue(
                    lambda q, cc=c, b=wt[c % 2]: q.dma_start(
                        out=b[:], in_=w2T[cc * 128:(cc + 1) * 128, :]), gate=gate)
                self.wtok(nc.tensor, wtok)
                for hn in range(2):
                    pb = pmm[hn]
                    if c >= 1:
                        self.wait(nc.tensor, "ev", evv[c - 1])
                    for nn in range(2):
                        _ti = nc.tensor.matmul(
                            pb[:, nn * 512:(nn + 1) * 512], self.c_h128[:],
                            wt[c % 2][:, hn * 1024 + nn * 512: hn * 1024 + (nn + 1) * 512],
                            start=True, stop=True)
                    mmv = self.inc(_ti, "mm", 1)
                    self.wait(nc.vector, "mm", mmv)
                    if c >= 2 and hn == 0:
                        self.wtok(nc.vector, otoks[c - 2])
                    ins = nc.vector.tensor_copy(
                        ev[c % 2][:, hn * 1024:(hn + 1) * 1024], pb[:])
                    evv[c] = self.inc(ins, "ev", 1)
                mmc[c] = self.cnt["mm"]
                otoks[c] = ost.issue(
                    lambda q, cc=c, b=ev[c % 2]: q.dma_start(
                        out=w2y_d[cc * 128:(cc + 1) * 128, :], in_=b[:]),
                    gate=(self.sems["ev"], evv[c]))
            self.barrier()
            st1.__exit__(None, None, None)

            yst = self.stream("Lyh", "sync", 2)
            yts = [self.sb(f"fw_y{i}", [64, D], BF16, st) for i in range(2)]
            ev2 = [self.sb(f"fw_f{i}", [64, D], BF16, st) for i in range(2)]
            p2 = [self.ps(f"fw_q{i}", [64, D], F32, st) for i in range(2)]
            w2y_r = w2y_d.rearrange("(c p) d -> p c d", p=128)
            w2p_r = w2p_d.rearrange("(c p) d -> p c d", p=128)
            scv2 = {}
            otok2 = {}
            mmp = {}
            for p in range(128):
                gate = (self.sems["mm"], mmp[p - 2]) if p >= 2 else None
                ytok = yst.issue(
                    lambda q, pp=p, b=yts[p % 2]: q.dma_start(
                        out=b[:], in_=w2y_r[pp:pp + 1].squeeze(0)), gate=gate)
                self.wtok(nc.tensor, ytok)
                pb = p2[p % 2]
                if p >= 2:
                    self.wait(nc.tensor, "sc", scv2[p - 2])
                for nn in range(4):
                    _ti = nc.tensor.matmul(pb[:, nn * 512:(nn + 1) * 512], self.c_h64[:],
                                     yts[p % 2][:, nn * 512:(nn + 1) * 512],
                                     start=True, stop=True)
                mmv = self.inc(_ti, "mm", 1)
                mmp[p] = mmv
                self.wait(nc.scalar, "mm", mmv)
                if p >= 2:
                    self.wtok(nc.scalar, otok2[p - 2])
                ins = nc.scalar.activation(out=ev2[p % 2][:], in_=pb[:],
                                           func=AF.Identity, scale=INVN_H)
                scv2[p] = self.inc(ins, "sc", 1)
                otok2[p] = ost.issue(
                    lambda q, pp=p, b=ev2[p % 2]: q.dma_start(
                        out=w2p_r[pp:pp + 1].squeeze(0), in_=b[:]),
                    gate=(self.sems["sc"], scv2[p]))

    def ffn(self, tag, xin_d, w1T, w3T, w2p_d, hid_d, xout_d, ln_row_in, stats_row):
        nc = self.nc
        st = contextlib.ExitStack()
        with st:
            xln = self.sb(f"ff{tag}_xln", [128, DC, TC], BF16, st)
            rows = self.sb(f"ff{tag}_rows", [1, 2, TC], F32, st)
            stA = contextlib.ExitStack()
            stA.__enter__()
            xt = [self.sb(f"ff{tag}_x{i}", [128, DC, 512], F32, stA) for i in range(2)]
            pbc = self.ps(f"ff{tag}_bc", [128, 1024], F32, stA)
            xst = self.stream("Lx", "sync", 3)
            rld = self.stream("Lyh", "sync", 2)
            rtok = rld.issue(lambda q: q.dma_start(
                out=rows[:], in_=self.lnrows_d[ln_row_in:ln_row_in + 2, :]))
            self.wtok(nc.tensor, rtok)
            xin_r = xin_d.rearrange("dc p t -> p dc t")
            xtoks = {}
            apv = {}
            for ci in range(2):
                xtoks[ci] = xst.issue(
                    lambda q, c=ci, b=xt[ci % 2]: q.dma_start(
                        out=b[:], in_=xin_r[:, :, c * 512:(c + 1) * 512]))
            for ci in range(4):
                tsl = slice(ci * 512, (ci + 1) * 512)
                if ci >= 1:
                    self.wait(nc.tensor, "ev")
                _ti = self.bcast(pbc[:, 0:512], rows[:, 0, tsl])
                _ti = self.bcast(pbc[:, 512:1024], rows[:, 1, tsl])
                mmv = self.inc(_ti, "mm", 1)
                self.wait(nc.vector, "mm", mmv)
                self.wtok(nc.vector, xtoks[ci])
                mb = pbc[:, 0:512].unsqueeze(1).broadcast_to([128, DC, 512])
                rb = pbc[:, 512:1024].unsqueeze(1).broadcast_to([128, DC, 512])
                nc.vector.tensor_tensor(xt[ci % 2][:], xt[ci % 2][:], mb, ALU.subtract)
                ins = nc.vector.tensor_tensor(xln[:, :, tsl], xt[ci % 2][:], rb,
                                              ALU.mult)
                apv[ci] = self.inc(ins, "ev", 1)
                if ci + 2 < 4:
                    xtoks[ci + 2] = xst.issue(
                        lambda q, c=ci + 2, b=xt[ci % 2]: q.dma_start(
                            out=b[:], in_=xin_r[:, :, c * 512:(c + 1) * 512]),
                        gate=(self.sems["ev"], apv[ci]))
            self.wait(nc.tensor, "ev")
            stA.__exit__(None, None, None)

            w1b = [self.sb(f"ff{tag}_w1{i}", [128, DC, 128], BF16, st) for i in range(2)]
            w3b = [self.sb(f"ff{tag}_w3{i}", [128, DC, 128], BF16, st) for i in range(2)]
            gat = self.sb(f"ff{tag}_g", [128, TC], F32, st)
            hb = [self.sb(f"ff{tag}_h{i}", [128, TC], BF16, st) for i in range(2)]
            pg = [self.ps(f"ff{tag}_pg{i}", [128, 512], F32, st) for i in range(4)]
            pl = [self.ps(f"ff{tag}_pl{i}", [128, 512], F32, st) for i in range(4)]
            wst = self.stream("Lw", "gpsimd", 3)
            w3st = self.stream("Lw3", "gpsimd", 2)
            hst = self.stream("Sy", "sync", 4)
            w1_r = w1T.rearrange("(dc p) f -> p dc f", p=128)
            w3_r = w3T.rearrange("(dc p) f -> p dc f", p=128)
            hid_w = hid_d.rearrange("(m p) t -> p m t", p=128)
            wtoks = {}
            w3toks = {}
            wmm = {}
            htoks = {}
            scv_m = {}
            evv_m = {}

            def wload(m):
                gate = (self.sems["mm"],
                        wmm[m - 2] if m >= 2 else self.cnt["mm"])
                wtoks[m] = wst.issue(
                    lambda q, mm_=m, b=w1b[m % 2]: q.dma_start(
                        out=b[:], in_=w1_r[:, :, mm_ * 128:(mm_ + 1) * 128]), gate=gate)
                w3toks[m] = w3st.issue(
                    lambda q, mm_=m, b=w3b[m % 2]: q.dma_start(
                        out=b[:], in_=w3_r[:, :, mm_ * 128:(mm_ + 1) * 128]), gate=gate)

            wload(0)
            wload(1)
            for m in range(64):
                self.wtok(nc.tensor, wtoks[m])
                self.wtok(nc.tensor, w3toks[m])
                if m >= 1:
                    self.wait(nc.tensor, "sc", scv_m[m - 1])
                for dc in range(DC):
                    for ci in range(4):
                        _ti = nc.tensor.matmul(pg[ci][:], w1b[m % 2][:, dc, :],
                                               xln[:, dc, ci * 512:(ci + 1) * 512],
                                               start=(dc == 0), stop=(dc == DC - 1))
                gmm = self.inc(_ti, "mm", 1)
                if m >= 1:
                    self.wait(nc.tensor, "ev", evv_m[m - 1])
                for dc in range(DC):
                    for ci in range(4):
                        _ti = nc.tensor.matmul(pl[ci][:], w3b[m % 2][:, dc, :],
                                               xln[:, dc, ci * 512:(ci + 1) * 512],
                                               start=(dc == 0), stop=(dc == DC - 1))
                lmm = self.inc(_ti, "mm", 1)
                self.wait(nc.scalar, "mm", gmm)
                for ci in range(4):
                    ins = nc.scalar.activation(out=gat[:, ci * 512:(ci + 1) * 512],
                                               in_=pg[ci][:], func=AF.Silu)
                scv_m[m] = self.inc(ins, "sc", 1)
                self.wait(nc.vector, "mm", lmm)
                self.wait(nc.vector, "sc", scv_m[m])
                if m >= 2:
                    self.wtok(nc.vector, htoks[m - 2])
                for ci in range(4):
                    tsl = slice(ci * 512, (ci + 1) * 512)
                    ins = nc.vector.tensor_tensor(hb[m % 2][:, tsl], gat[:, tsl],
                                                  pl[ci][:], ALU.mult)
                evv_m[m] = self.inc(ins, "ev", 1)
                wmm[m] = self.cnt["mm"]
                if m + 2 < 64:
                    wload(m + 2)
                htoks[m] = hst.issue(
                    lambda q, mm_=m, b=hb[m % 2]: q.dma_start(
                        out=hid_w[:, mm_, :], in_=b[:]),
                    gate=(self.sems["ev"], evv_m[m]))
        self.barrier()
        self.contract8k(f"{tag}c", w2p_d, True, hid_d, xin_d, None, xout_d,
                        stats_row=stats_row)

    def p3_gather(self, x2_d, xg_in, xg_out, ln_row):
        nc = self.nc
        st = contextlib.ExitStack()
        with st:
            xt = [self.sb(f"p3_x{i}", [128, DC, 512], F32, st) for i in range(2)]
            ob = [self.sb(f"p3_o{i}", [128, DC, 512], BF16, st) for i in range(2)]
            rows = self.sb("p3_rows", [1, 2, TC], F32, st)
            pbc = self.ps("p3_bc", [128, 1024], F32, st)
            xst = self.stream("Lx", "sync", 3)
            rld = self.stream("Lyh", "sync", 2)
            sp = self.stream("Sxa", "gpsimd", 4)
            rtok = rld.issue(lambda q: q.dma_start(
                out=rows[:], in_=self.lnrows_d[ln_row:ln_row + 2, :]))
            self.wtok(nc.tensor, rtok)
            x2_r = x2_d.rearrange("dc p t -> p dc t")
            self.cc_base_ag = self.cnt["cc"]
            xtoks = {}
            ov = {}
            otoks = {}
            for ci in range(2):
                xtoks[ci] = xst.issue(
                    lambda q, c=ci, b=xt[ci % 2]: q.dma_start(
                        out=b[:], in_=x2_r[:, :, c * 512:(c + 1) * 512]))
            for ci in range(4):
                tsl = slice(ci * 512, (ci + 1) * 512)
                if ci >= 1:
                    self.wait(nc.tensor, "ev")
                _ti = self.bcast(pbc[:, 0:512], rows[:, 0, tsl])
                _ti = self.bcast(pbc[:, 512:1024], rows[:, 1, tsl])
                mmv = self.inc(_ti, "mm", 1)
                self.wait(nc.vector, "mm", mmv)
                self.wtok(nc.vector, xtoks[ci])
                if ci >= 2:
                    self.wtok(nc.vector, otoks[ci - 2])
                mb = pbc[:, 0:512].unsqueeze(1).broadcast_to([128, DC, 512])
                rb = pbc[:, 512:1024].unsqueeze(1).broadcast_to([128, DC, 512])
                nc.vector.tensor_tensor(xt[ci % 2][:], xt[ci % 2][:], mb, ALU.subtract)
                ins = nc.vector.tensor_tensor(ob[ci % 2][:], xt[ci % 2][:], rb,
                                              ALU.mult)
                ov[ci] = self.inc(ins, "ev", 1)
                otoks[ci] = sp.issue(
                    lambda q, c=ci, b=ob[ci % 2]: q.dma_start(
                        out=xg_in[c].rearrange("(dc p) t -> p dc t", p=128),
                        in_=b[:]),
                    gate=(self.sems["ev"], ov[ci]))
                if ci + 2 < 4:
                    xtoks[ci + 2] = xst.issue(
                        lambda q, c=ci + 2, b=xt[ci % 2]: q.dma_start(
                            out=b[:], in_=x2_r[:, :, c * 512:(c + 1) * 512]),
                        gate=(self.sems["ev"], ov[ci]))
                self.wtok(nc.gpsimd, otoks[ci])
                ins = nc.gpsimd.collective_compute(
                    "AllGather", ALU.bypass, replica_groups=GROUPS,
                    ins=[xg_in[ci].opt()], outs=[xg_out[ci].opt()])
                self.inc(ins, "cc", 1)
        self.barrier(skip=("cc",))

    def p4_qkv(self, xg_out, wq_own, wk_own, wv_own, cos_t, sin_t, q_d):
        nc = self.nc
        self.kv_sb = self.carry("kv", [128, 2, T_], BF16)
        st = contextlib.ExitStack()
        with st:
            trig = self.sb("p4_trig", [128, T_], F32, st)
            wq = self.sb("p4_wq", [128, DC, 512], BF16, st)
            wk = self.sb("p4_wk", [128, DC, 128], BF16, st)
            wv = self.sb("p4_wv", [128, DC, 128], BF16, st)
            xg = [self.sb(f"p4_xg{i}", [128, DC, 512], BF16, st) for i in range(2)]
            rt1 = self.sb("p4_r1", [64, 512], F32, st)
            rt2 = self.sb("p4_r2", [64, 512], F32, st)
            qs = [self.sb(f"p4_qs{i}", [128, 512], BF16, st) for i in range(4)]
            vtmp = self.sb("p4_vtmp", [128, 512], BF16, st)
            pm = [self.ps(f"p4_pm{i}", [128, 512], F32, st) for i in range(2)]
            ptr = self.ps("p4_ptr", [128, 512], BF16, st)
            wst = self.stream("Lw", "gpsimd", 3)
            xst = self.stream("Lx", "sync", 3)
            qsp = self.stream("Sxa", "gpsimd", 4)
            cst = self.stream("Lyh", "sync", 2)

            ct = cst.issue(lambda q: q.dma_start(out=trig[0:64, :], in_=cos_t))
            stk = cst.issue(lambda q: q.dma_start(out=trig[64:128, :], in_=sin_t))
            wqt = wst.issue(lambda q: q.dma_start(
                out=wq[:], in_=wq_own.rearrange("(dc p) f -> p dc f", p=128)))
            wkt = wst.issue(lambda q: q.dma_start(
                out=wk[:], in_=wk_own.rearrange("(dc p) f -> p dc f", p=128)))
            wvt = wst.issue(lambda q: q.dma_start(
                out=wv[:], in_=wv_own.rearrange("(dc p) f -> p dc f", p=128)))
            for t in (ct, stk, wqt, wkt, wvt):
                self.wtok(nc.tensor, t)
                self.wtok(nc.vector, t)

            rt_order = [(tq, r) for tq in range(4) for r in range(4)]
            xtoks = {}
            qtoks = {}
            mmrt = {}
            trv = {}

            def xgload(rt):
                tq, r = rt_order[rt]
                gates = [(self.sems["cc"], self.cc_base_ag + tq + 1)]
                if rt >= 2:
                    gates.append((self.sems["mm"], mmrt[rt - 2]))
                xtoks[rt] = xst.issue(
                    lambda q, rr=r, t0=tq, b=xg[rt % 2]: q.dma_start(
                        out=b[:],
                        in_=xg_out[t0].rearrange("(r dc p) t -> r p dc t",
                                                 r=4, p=128)[rr]),
                    gate=gates)

            xgload(0)
            xgload(1)
            for rt in range(16):
                tq, r = rt_order[rt]
                gcol = r * 2048 + tq * 512
                csl = slice(gcol, gcol + 512)
                self.wtok(nc.tensor, xtoks[rt])
                xb = xg[rt % 2]
                for tgt in range(6):
                    pb = pm[tgt % 2]
                    if rt * 6 + tgt >= 2:
                        self.wait(nc.tensor, "ev")
                    if tgt < 4:
                        wsrc = lambda dc: wq[:, dc, tgt * 128:(tgt + 1) * 128]
                    elif tgt == 4:
                        wsrc = lambda dc: wk[:, dc, :]
                    else:
                        wsrc = lambda dc: wv[:, dc, :]
                    for dc in range(DC):
                        _ti = nc.tensor.matmul(pb[:], wsrc(dc), xb[:, dc, :],
                                         start=(dc == 0), stop=(dc == DC - 1))
                    mmv = self.inc(_ti, "mm", 1)
                    self.wait(nc.vector, "mm", mmv)
                    if tgt < 5:
                        if tgt < 4:
                            if rt >= 1:
                                self.wtok(nc.vector, qtoks[(rt - 1) * 4 + tgt])
                            dst = qs[tgt][:]
                        else:
                            dst = self.kv_sb[:, 0, csl]
                        nc.vector.tensor_tensor(rt1[:], pb[0:64, :], trig[0:64, csl],
                                                ALU.mult)
                        nc.vector.tensor_tensor(rt2[:], pb[64:128, :], trig[64:128, csl],
                                                ALU.mult)
                        nc.vector.tensor_tensor(dst[0:64, :], rt1[:], rt2[:],
                                                ALU.subtract)
                        nc.vector.tensor_tensor(rt1[:], pb[0:64, :], trig[64:128, csl],
                                                ALU.mult)
                        nc.vector.tensor_tensor(rt2[:], pb[64:128, :], trig[0:64, csl],
                                                ALU.mult)
                        ins = nc.vector.tensor_tensor(dst[64:128, :], rt1[:], rt2[:],
                                                      ALU.add)
                        vv = self.inc(ins, "ev", 1)
                        if tgt < 4:
                            qtoks[rt * 4 + tgt] = qsp.issue(
                                lambda q, u=tgt, c=csl, b=qs[tgt]: q.dma_start(
                                    out=q_d[u, :, c], in_=b[:]),
                                gate=(self.sems["ev"], vv))
                    else:
                        if rt >= 1:
                            self.wait(nc.vector, "mm", trv[rt - 1])
                        ins = nc.vector.tensor_copy(vtmp[:], pb[:])
                        vv = self.inc(ins, "ev", 1)
                        nc.tensor.wait_ge(self.sems["ev"], vv)
                        for k in range(4):
                            _ti = nc.tensor.transpose(ptr[:, k * 128:(k + 1) * 128],
                                                vtmp[:, k * 128:(k + 1) * 128],
                                                self.c_ident_bf[:])
                        trv[rt] = self.inc(_ti, "mm", 1)
                        self.wait(nc.vector, "mm", trv[rt])
                        ins = nc.vector.tensor_copy(self.kv_sb[:, 1, csl], ptr[:])
                        self.inc(ins, "ev", 1)
                mmrt[rt] = self.cnt["mm"]
                if rt + 2 < 16:
                    xgload(rt + 2)

    def p5_attn(self, q_d, attn_d):
        nc = self.nc
        if "kv_dump" in self.taps:
            kvd = self.dscratch("kv_dump", [128, 2, T_], BF16)
            sp0 = self.stream("Sxa", "gpsimd", 4)
            sp0.issue(lambda q: q.dma_start(out=kvd, in_=self.kv_sb[:]),
                      gate=self.tok("ev"))
            self.barrier()
        st = contextlib.ExitStack()
        with st:
            qp = [self.sb(f"p5_qp{i}", [128, 512], BF16, st) for i in range(2)]
            srow = self.sb("p5_srow", [128, 2, T_], F32, st)
            P = self.sb("p5_P", [128, 4, T_], BF16, st)
            pt = [self.sb(f"p5_pt{i}", [128, 512], BF16, st) for i in range(2)]
            ab = self.sb("p5_ab", [128, 1024], F32, st)
            rws = self.sb("p5_rws", [128, 4], F32, st)
            ob = [self.sb(f"p5_ob{i}", [128, 512], BF16, st) for i in range(2)]
            ps_s = [self.ps(f"p5_s{i}", [128, 1024], F32, st) for i in range(2)]
            ps_t = [self.ps(f"p5_t{i}", [128, 512], BF16, st) for i in range(2)]
            ps_o = self.ps("p5_o", [128, 512], F32, st)
            qld = self.stream("Lyh", "sync", 2)
            asp = self.stream("Sxa", "gpsimd", 4)

            ins = nc.vector.memset(rws[:], 0.0)
            rwsv = self.inc(ins, "ev", 1)
            self.wait(nc.scalar, "ev", rwsv)

            otoks = {}
            qtoks = {}
            qmm = {}

            def qload(g):
                u_, p_ = divmod(g, 16)
                gate = (self.sems["mm"],
                        qmm[g - 1] if (g - 1) in qmm else self.cnt["mm"])
                qtoks[g] = qld.issue(
                    lambda q, uu=u_, pp=p_, b=qp[g % 2]: q.dma_start(
                        out=b[:], in_=q_d[uu, :, 512 * pp:512 * (pp + 1)]),
                    gate=gate)

            qload(0)
            qload(1)
            pair_i = 0
            gc = 0
            evac_pair = {}
            exp_scv = {}
            prev_pv_mm = 0
            prev_sov = 0
            for u in range(4):
                ins = nc.gpsimd.tensor_scalar(
                    ab[:], self.c_dwide[:], self.c_slope[:, u:u + 1], None, ALU.mult)
                abv = self.inc(ins, "gp", 1)
                for p in range(16):
                    g = u * 16 + p
                    E = 512 * (p + 1)
                    nkc = E // 128
                    self.wtok(nc.tensor, qtoks[g])
                    for c in range(4):
                        q0 = 128 * c
                        sr = srow[:, gc % 2, :]
                        pairs = []
                        k0 = 0
                        while k0 < E:
                            pairs.append((k0, min(1024, E - k0)))
                            k0 += 1024
                        gpv = 0
                        for (pk, (k0, w)) in enumerate(pairs):
                            pb = ps_s[pair_i % 2]
                            if pair_i >= 2:
                                self.wait(nc.tensor, "ev", evac_pair[pair_i - 2])
                            for kk in range(w // 512):
                                _ti = nc.tensor.matmul(
                                    pb[:, kk * 512:(kk + 1) * 512],
                                    qp[g % 2][:, q0:q0 + 128],
                                    self.kv_sb[:, 0, k0 + kk * 512:k0 + (kk + 1) * 512],
                                    start=True, stop=True)
                            mmv = self.inc(_ti, "mm", 1)
                            self.wait(nc.vector, "mm", mmv)
                            if pk == 0 and gc >= 2:
                                self.wait(nc.vector, "sc", exp_scv[gc - 2])
                            dcol = _DMAP[k0 - (512 * p + 128 * c)] + u * 64
                            ins = nc.vector.tensor_scalar(
                                sr[:, k0:k0 + w], pb[:, 0:w], INVSQ,
                                self.c_dbias[:, dcol:dcol + 1], ALU.mult, ALU.add)
                            evac_pair[pair_i] = self.inc(ins, "ev", 1)
                            self.wait(nc.gpsimd, "ev", evac_pair[pair_i])
                            ins = nc.gpsimd.tensor_tensor(
                                sr[:, k0:k0 + w], sr[:, k0:k0 + w], ab[:, 0:w],
                                ALU.add)
                            if pk == len(pairs) - 1:
                                msk = (self.c_maskw[:, c, :] if w == 1024
                                       else self.c_maskn[:, c, :])
                                ins = nc.gpsimd.tensor_tensor(
                                    sr[:, k0:k0 + w], sr[:, k0:k0 + w], msk, ALU.add)
                            gpv = self.inc(ins, "gp", 1)
                            pair_i += 1
                        self.wait(nc.vector, "gp", gpv)
                        ins = nc.vector.tensor_reduce(
                            rws[:, 0:1], sr[:, 0:E], axis=mybir.AxisListType.X,
                            op=ALU.max, negate=True)
                        nmv = self.inc(ins, "ev", 1)
                        self.wait(nc.scalar, "ev", nmv)
                        self.wait(nc.scalar, "mm", prev_pv_mm)
                        nc.scalar.activation(
                            out=P[:, c, 0:E], in_=sr[:, 0:E], func=AF.Exp,
                            bias=rws[:, 0:1], accum_out=rws[:, 1:2])
                        nc.scalar.activation(out=rws[:, 2:3], in_=rws[:, 1:2],
                                             func=AF.Ln)
                        nc.scalar.activation(out=rws[:, 3:4], in_=rws[:, 2:3],
                                             func=AF.Exp, scale=-1.0)
                        ins = nc.scalar.activation(
                            out=P[:, c, 0:E], in_=P[:, c, 0:E], func=AF.Identity,
                            scale=rws[:, 3:4])
                        exp_scv[gc] = self.inc(ins, "sc", 1)
                        gc += 1

                    qmm[g] = self.cnt["mm"]
                    if g + 2 < 64:
                        qload(g + 2)
                    nc.tensor.wait_ge(self.sems["sc"], exp_scv[gc - 1])
                    ptv = {}
                    for kc in range(nkc + 1):
                        if kc < nkc:
                            pbt = ps_t[kc % 2]
                            if kc >= 2:
                                self.wait(nc.tensor, "ev", ptv[kc - 2])
                            for c4 in range(4):
                                _ti = nc.tensor.transpose(
                                    pbt[:, c4 * 128:(c4 + 1) * 128],
                                    P[:, c4, kc * 128:(kc + 1) * 128],
                                    self.c_ident_bf[:])
                            mmv = self.inc(_ti, "mm", 1)
                            self.wait(nc.vector, "mm", mmv)
                            ins = nc.vector.tensor_copy(pt[kc % 2][:], pbt[:])
                            ptv[kc] = self.inc(ins, "ev", 1)
                        if kc >= 1:
                            k1 = kc - 1
                            nc.tensor.wait_ge(self.sems["ev"], ptv[k1])
                            if k1 == 0:
                                nc.tensor.wait_ge(self.sems["sc"], prev_sov)
                            _ti = nc.tensor.matmul(
                                ps_o[:], self.kv_sb[:, 1, k1 * 128:(k1 + 1) * 128],
                                pt[k1 % 2][:],
                                start=(k1 == 0), stop=(k1 == nkc - 1))
                    mmv = self.inc(_ti, "mm", 1)
                    prev_pv_mm = mmv
                    self.wait(nc.scalar, "mm", mmv)
                    if g >= 2:
                        self.wtok(nc.scalar, otoks[g - 2])
                    ins = nc.scalar.activation(out=ob[p % 2][:], in_=ps_o[:],
                                               func=AF.Identity)
                    sov = self.inc(ins, "sc", 1)
                    prev_sov = sov
                    otoks[g] = asp.issue(
                        lambda q, uu=u, pp=p, b=ob[p % 2]: q.dma_start(
                            out=attn_d[uu * 128:(uu + 1) * 128,
                                       512 * pp:512 * (pp + 1)], in_=b[:]),
                        gate=(self.sems["sc"], sov))

    def p6_wo(self, wo_own, attn_d, rs_in, rs_out):
        nc = self.nc
        st = contextlib.ExitStack()
        with st:
            wo = self.sb("p6_wo", [128, 4, D], BF16, st)
            atb = [self.sb(f"p6_a{i}", [128, 4, 512], BF16, st) for i in range(2)]
            stg = [self.sb(f"p6_s{i}", [128, 512], BF16, st) for i in range(2)]
            pm = [self.ps(f"p6_pm{i}", [128, 512], F32, st) for i in range(2)]
            wst = self.stream("Lw", "gpsimd", 3)
            ast = self.stream("Lx", "sync", 3)
            sp = self.stream("Sxa", "gpsimd", 4)
            wo_r = wo_own.rearrange("(ec p) d -> p ec d", p=128)
            at_r = attn_d.rearrange("(ec p) t -> p ec t", p=128)
            wtok = wst.issue(lambda q: q.dma_start(out=wo[:], in_=wo_r))
            self.wtok(nc.tensor, wtok)
            self.cc_base_rs = self.cnt["cc"]
            order = [(tcq, tb) for tcq in range(4) for tb in range(4)]
            atoks = {}
            sptk = {}
            scv_of = {}
            gidx = 0

            def aload(tg):
                tcq, tb = order[tg]
                t = tb * 4 + tcq
                gate = (self.sems["mm"], self.cnt["mm"]) if tg >= 2 else None
                atoks[tg] = ast.issue(
                    lambda q, tt_=t, b=atb[tg % 2]: q.dma_start(
                        out=b[:], in_=at_r[:, :, tt_ * 512:(tt_ + 1) * 512]),
                    gate=gate)

            aload(0)
            aload(1)
            for tg in range(16):
                tcq, tb = order[tg]
                self.wtok(nc.tensor, atoks[tg])
                chunk_sp = []
                for dci in range(DC):
                    pb = pm[gidx % 2]
                    if gidx >= 2:
                        self.wait(nc.tensor, "sc", scv_of[gidx - 2])
                    for ec in range(4):
                        _ti = nc.tensor.matmul(pb[:], wo[:, ec, dci * 128:(dci + 1) * 128],
                                               atb[tg % 2][:, ec, :],
                                               start=(ec == 0), stop=(ec == 3))
                    mmv = self.inc(_ti, "mm", 1)
                    self.wait(nc.scalar, "mm", mmv)
                    if gidx >= 2:
                        self.wtok(nc.scalar, sptk[gidx - 2])
                    ins = nc.scalar.activation(out=stg[gidx % 2][:], in_=pb[:],
                                               func=AF.Identity)
                    scv_of[gidx] = self.inc(ins, "sc", 1)
                    sptk[gidx] = sp.issue(
                        lambda q, c=tcq, t=tb, d=dci, b=stg[gidx % 2]: q.dma_start(
                            out=rs_in[c].rearrange("(tb dc p) t -> tb p dc t",
                                                   tb=4, p=128)[t, :, d, :],
                            in_=b[:]),
                        gate=(self.sems["sc"], scv_of[gidx]))
                    chunk_sp.append(sptk[gidx])
                    gidx += 1
                if tg + 2 < 16:
                    aload(tg + 2)
                if tb == 3:
                    for tk in chunk_sp[-4:]:
                        self.wtok(nc.gpsimd, tk)
                    ins = nc.gpsimd.collective_compute(
                        "ReduceScatter", ALU.add, replica_groups=GROUPS,
                        ins=[rs_in[tcq].opt()], outs=[rs_out[tcq].opt()])
                    self.inc(ins, "cc", 1)
        self.barrier(skip=("cc",))

    def p6b_x3(self, x2_d, rs_out, x3_d, stats_row):
        nc = self.nc
        st = contextlib.ExitStack()
        with st:
            xt = [self.sb(f"p6b_x{i}", [128, DC, 512], F32, st) for i in range(2)]
            rt = [self.sb(f"p6b_r{i}", [128, DC, 512], BF16, st) for i in range(2)]
            xbf = self.sb("p6b_xb", [128, DC, 512], BF16, st)
            sqb = self.sb("p6b_sq", [128, DC, 512], BF16, st)
            pstq = [self.ps(f"p6b_ps{i}", [1, 512], F32, st) for i in range(2)]
            lrow = self.sb("p6b_rows", [1, 4, 512], F32, st)
            xst = self.stream("Lx", "sync", 3)
            rst = self.stream("Lr", "sync", 3)
            sp = self.stream("Sxa", "gpsimd", 4)
            x2_r = x2_d.rearrange("dc p t -> p dc t")
            xtoks = {}
            rtoks = {}
            ov = {}
            sptoks = {}
            statmm = {}
            rowtoks = {}

            def loads(ci, gate=None):
                xtoks[ci] = xst.issue(
                    lambda q, c=ci, b=xt[ci % 2]: q.dma_start(
                        out=b[:], in_=x2_r[:, :, c * 512:(c + 1) * 512]), gate=gate)
                gates = [(self.sems["cc"], self.cc_base_rs + ci + 1)]
                if gate is not None:
                    gates.extend(gate if isinstance(gate, list) else [gate])
                rtoks[ci] = rst.issue(
                    lambda q, c=ci, b=rt[ci % 2]: q.dma_start(
                        out=b[:],
                        in_=rs_out[c].rearrange("(dc p) t -> p dc t", p=128)),
                    gate=gates)

            loads(0)
            loads(1)
            for ci in range(4):
                tsl = slice(ci * 512, (ci + 1) * 512)
                self.wtok(nc.vector, xtoks[ci])
                self.wtok(nc.vector, rtoks[ci])
                if ci >= 2:
                    self.wtok(nc.vector, sptoks[ci - 2])
                if ci >= 1:
                    self.wait(nc.vector, "mm", statmm[ci - 1])
                nc.vector.tensor_tensor(xt[ci % 2][:], xt[ci % 2][:], rt[ci % 2][:],
                                        ALU.add)
                ins = nc.vector.tensor_copy(xbf[:], xt[ci % 2][:])
                ov[ci] = self.inc(ins, "ev", 1)
                sptoks[ci] = sp.issue(
                    lambda q, c=ci, b=xt[ci % 2]: q.dma_start(
                        out=x3_d.rearrange("dc p t -> p dc t")[:, :, c * 512:(c + 1) * 512],
                        in_=b[:]),
                    gate=(self.sems["ev"], ov[ci]))
                self.wait(nc.scalar, "ev", ov[ci])
                if ci >= 1:
                    self.wait(nc.scalar, "mm", statmm[ci - 1])
                ins = nc.scalar.activation(out=sqb[:], in_=xt[ci % 2][:],
                                           func=AF.Square)
                sqv = self.inc(ins, "sc", 1)
                if ci + 2 < 4:
                    loads(ci + 2, gate=[sptoks[ci], (self.sems["sc"], sqv)])
                nc.tensor.wait_ge(self.sems["sc"], sqv)
                nc.tensor.wait_ge(self.sems["ev"], ov[ci])
                for dc in range(DC):
                    _ti = nc.tensor.matmul(pstq[0][:], self.c_ones_bf[:], xbf[:, dc, :],
                                           start=(dc == 0), stop=(dc == DC - 1))
                for dc in range(DC):
                    _ti = nc.tensor.matmul(pstq[1][:], self.c_ones_bf[:], sqb[:, dc, :],
                                           start=(dc == 0), stop=(dc == DC - 1))
                statmm[ci] = self.inc(_ti, "mm", 1)
                self.wait(nc.vector, "mm", statmm[ci])
                if ci >= 1:
                    self.wtok(nc.vector, rowtoks[ci - 1])
                scv = self.ln_rows("ln", pstq[0][:], pstq[1][:],
                                   lrow[:, 0, :], lrow[:, 1, :],
                                   lrow[:, 2, :], lrow[:, 3, :])
                rowtoks[ci] = sp.issue(
                    lambda q, c=ci: q.dma_start(
                        out=self.lnrows_d[stats_row:stats_row + 2,
                                          c * 512:(c + 1) * 512],
                        in_=lrow[:, 0:2, :]),
                    gate=(self.sems["sc"], scv))

    def p8_out(self, x4_d, out):
        nc = self.nc
        st = contextlib.ExitStack()
        with st:
            xt = [self.sb(f"p8_x{i}", [128, DC, 128], F32, st) for i in range(2)]
            obuf = [self.sb(f"p8_o{i}", [128, D], F32, st) for i in range(2)]
            ptr = [self.ps(f"p8_p{i}", [128, 512], F32, st) for i in range(2)]
            xst = self.stream("Lx", "sync", 3)
            ost = self.stream("Sout", "sync", 2)
            x4_r = x4_d.rearrange("dc p t -> p dc t")
            xtoks = {}
            otoks = {}
            evv = {}
            mmg = {}
            for tg in range(2):
                xtoks[tg] = xst.issue(
                    lambda q, t=tg, b=xt[tg % 2]: q.dma_start(
                        out=b[:], in_=x4_r[:, :, t * 128:(t + 1) * 128]))
            for tg in range(16):
                self.wtok(nc.tensor, xtoks[tg])
                if tg >= 2:
                    self.wtok(nc.tensor, otoks[tg - 2])
                for half in range(4):
                    pb = ptr[half % 2]
                    gl = tg * 4 + half
                    if gl >= 2:
                        self.wait(nc.tensor, "ev", evv[gl - 2])
                    for k in range(4):
                        dc = half * 4 + k
                        _ti = nc.tensor.transpose(pb[:, k * 128:(k + 1) * 128],
                                            xt[tg % 2][:, dc, :], self.c_ident_f[:])
                    mmv = self.inc(_ti, "mm", 1)
                    self.wait(nc.vector, "mm", mmv)
                    ins = nc.vector.tensor_copy(
                        obuf[tg % 2][:, half * 512:(half + 1) * 512], pb[:])
                    evv[gl] = self.inc(ins, "ev", 1)
                mmg[tg] = self.cnt["mm"]
                if tg + 2 < 16:
                    xtoks[tg + 2] = xst.issue(
                        lambda q, t=tg + 2, b=xt[(tg + 2) % 2]: q.dma_start(
                            out=b[:], in_=x4_r[:, :, t * 128:(t + 1) * 128]),
                        gate=(self.sems["mm"], mmg[tg]))
                otoks[tg] = ost.issue(
                    lambda q, t=tg, b=obuf[tg % 2]: q.dma_start(
                        out=out[t * 128:(t + 1) * 128, :], in_=b[:]),
                    gate=(self.sems["ev"], evv[tg * 4 + 3]))


def _prep_inputs(inputs):
    x = np.asarray(inputs["L0_attn_a_x"], dtype=np.float32).reshape(B_ * T_, D)
    w_inT = np.ascontiguousarray(np.asarray(inputs["L0_attn_a_W_in"], dtype=np.float32).T)
    conv_w = np.ascontiguousarray(np.asarray(inputs["L0_attn_a_conv_w"], dtype=np.float32)
                                  .reshape(32, 128, 4).transpose(1, 0, 2))
    conv_b = np.ascontiguousarray(np.asarray(inputs["L0_attn_a_conv_b"], dtype=np.float32)
                                  .reshape(32, 128, 1).transpose(1, 0, 2))
    w_outT = np.ascontiguousarray(np.asarray(inputs["L0_attn_a_W_out"], dtype=np.float32).T)
    mlp1T = np.ascontiguousarray(np.asarray(inputs["L0_attn_a_W_mlp1"], dtype=np.float32).T)
    mlp2T = np.ascontiguousarray(np.asarray(inputs["L0_attn_a_W_mlp2"], dtype=np.float32).T)
    w1T = np.ascontiguousarray(np.asarray(inputs["L0_ffn_W1"], dtype=np.float32).T)
    w3T = np.ascontiguousarray(np.asarray(inputs["L0_ffn_W3"], dtype=np.float32).T)
    w2T = np.ascontiguousarray(np.asarray(inputs["L0_ffn_W2"], dtype=np.float32).T)
    wq = np.asarray(inputs["L0_attn_b_Wq"], dtype=np.float32)
    wk = np.asarray(inputs["L0_attn_b_Wk"], dtype=np.float32)
    wv = np.asarray(inputs["L0_attn_b_Wv"], dtype=np.float32)
    wo = np.asarray(inputs["L0_attn_b_Wo"], dtype=np.float32)
    consts = _host_consts()
    maps = []
    for c in range(8):
        b, g = divmod(c, 4)
        t0 = b * T_ + g * TC
        m = dict(consts)
        m["x_own"] = np.ascontiguousarray(x[t0:t0 + TC])
        xh = np.zeros((128, D), dtype=np.float32)
        if g > 0:
            xh[0:HALO] = x[t0 - HALO:t0]
        m["x_halo"] = xh
        m.update(w_inT=w_inT, conv_w=conv_w, conv_b=conv_b, w_outT=w_outT,
                 mlp1T=mlp1T, mlp2T=mlp2T, w1T=w1T, w3T=w3T, w2T=w2T)
        m["wq_own"] = np.ascontiguousarray(wq[:, 512 * g:512 * (g + 1)])
        m["wk_own"] = np.ascontiguousarray(wk[:, HDIM * g:HDIM * (g + 1)])
        m["wv_own"] = np.ascontiguousarray(wv[:, HDIM * g:HDIM * (g + 1)])
        m["wo_own"] = np.ascontiguousarray(wo[512 * g:512 * (g + 1), :])
        tt = _per_core_tables(g)
        m["slope"] = tt["slope"]
        m["dbias"] = tt["dbias"]
        maps.append(m)
    return maps


def _run(inputs, taps=(), phase_limit=99, trace=False):
    mk = MK(taps=taps)
    with mk.es:
        mk.build(phase_limit=phase_limit)
    maps = _prep_inputs(inputs)
    res = run_bass_kernel_spmd(mk.nc, maps, core_ids=list(range(8)), trace=trace)
    return res, mk.out_names


def kernel(**inputs):
    res, _ = _run(inputs)
    outp = np.empty((B_ * T_, D), dtype=np.float32)
    for c in range(8):
        b, g = divmod(c, 4)
        t0 = b * T_ + g * TC
        outp[t0:t0 + TC] = res.results[c]["out"]
    return outp.reshape(B_, T_, D)
